# version 10
# baseline (speedup 1.0000x reference)
import os
import numpy as np
import ml_dtypes
import concourse.bass as bass
import concourse.mybir as mybir
from concourse.bass_utils import run_bass_kernel_spmd
import concourse.bass_utils as _bu

_orig_run_command = _bu.run_command
def _run_command_ldw(argv, **kw):
    argv = ["--enable-ldw-opt=true" if a == "--enable-ldw-opt=false" else a
            for a in argv]
    return _orig_run_command(argv, **kw)
_bu.run_command = _run_command_ldw

F32 = mybir.dt.float32
BF16 = mybir.dt.bfloat16
U32 = mybir.dt.uint32
AF = mybir.ActivationFunctionType
ALU = mybir.AluOpType
AX = mybir.AxisListType
bfdt = ml_dtypes.bfloat16

B, L, D = 4, 4096, 1024
H, DH = 16, 64
E = 32
HS = 512
T = 2048
NT = 16
NG = 32
DC = 8
HC = 36
TB = 1024
NTB = T // TB
C_EXP = 12.0
NREL = 2047
NRELX = 2175
PAD = 255
GW = 2560
EPS_LN = 1e-5
ZSLOT = 12

DEBUG_STAGE = 1


def win(lt):
    i0t = 256 * lt
    return max(0, i0t - 1023), min(L, i0t + 1279)


class Sem:
    def __init__(self, nc, name):
        self.h = nc.semaphore(name).__enter__()
        self.n = 0
    def inc(self, instr, k=1):
        self.n += k
        if instr is not None:
            instr.then_inc(self.h, k)
        return self.n


_DRAIN_OPS = {
    "memset", "memzero", "tensor_copy", "tensor_tensor", "tensor_add",
    "tensor_sub", "tensor_mul", "tensor_max", "tensor_scalar",
    "tensor_single_scalar", "tensor_scalar_add", "tensor_scalar_sub",
    "tensor_scalar_mul", "tensor_scalar_max", "tensor_scalar_min",
    "tensor_relu", "tensor_reduce", "reduce_max", "reduce_sum",
    "scalar_tensor_tensor", "select", "reciprocal", "activation", "square",
    "sqrt", "mul", "copy", "add", "sign", "affine_select", "iota",
    "bn_stats", "bn_aggr", "transpose", "stream_shuffle",
}


class Eng:
    def __init__(self, real, auto_drain=False):
        object.__setattr__(self, "real", real)
        object.__setattr__(self, "auto_drain", auto_drain)
        object.__setattr__(self, "_last_was_op", False)
    def __getattr__(self, name):
        real = object.__getattribute__(self, "real")
        if real is None:
            return lambda *a, **k: None
        fn = getattr(real, name)
        if name in _DRAIN_OPS and object.__getattribute__(self, "auto_drain"):
            def wrapped(*a, **k):
                if object.__getattribute__(self, "_last_was_op"):
                    real.drain()
                object.__setattr__(self, "_last_was_op", True)
                return fn(*a, **k)
            return wrapped
        return fn
    def wait(self, sem, val):
        real = object.__getattribute__(self, "real")
        if real is not None and val > 0:
            real.wait_ge(sem.h, val)


def build_program():
    nc = bass.Bass("TRN2", target_bir_lowering=False,
                   detect_race_conditions=os.environ.get("KRACE", "0") == "1")

    def din(name, shape, dtype=F32):
        return nc.dram_tensor(name, shape, dtype, kind="ExternalInput")

    x_in = din("x", [NT, 128, D])
    wq_d = din("wq", [DC, 128, DC * 128])
    wk_d = din("wk", [DC, 128, DC * 128])
    wv_d = din("wv", [D, D])
    wo_d = din("wo", [D, D])
    bq_d = din("bq", [1, D]); bk_d = din("bk", [1, D])
    bv_d = din("bv", [1, D]); bo_d = din("bo", [1, D])
    relr_d = din("relr", [128, NRELX])
    wr_d = din("wr", [128, DC * E])
    rb_d = din("rb", [E, 1])
    wg_d = din("wg", [HC, 128, DC * 128])
    wu_d = din("wu", [HC, 128, DC * 128])
    wd_d = din("wd", [2, HC, 128, 512])

    out_d = nc.dram_tensor("out", [NT, 128, D], F32, kind="ExternalOutput")

    k_hbm = nc.dram_tensor("k_hbm", [DC, 128, L], BF16, kind="Internal")
    v_hbm = nc.dram_tensor("v_hbm", [NG, 128, D], BF16, kind="Internal")
    g_hbm = nc.dram_tensor("g_hbm", [2, 128, 2 * GW], BF16, kind="Internal")
    ag_in = nc.dram_tensor("ag_in", [128, DC * T], BF16, kind="Internal")
    ag_out = nc.dram_tensor("ag_out", [2, 128, DC * T], BF16, kind="Internal")

    ESZ = {F32: 4, BF16: 2, U32: 4}

    def sbn(name, shape, dtype):
        return nc.alloc_sbuf_tensor(name, list(shape), dtype)

    X      = sbn("X", [128, NT * D], BF16)
    identb = sbn("identb", [128, 128], BF16)
    identf = sbn("identf", [128, 128], F32)
    onesr  = sbn("onesr", [1, 512], BF16)
    onestf = sbn("onestf", [128, 256], F32)
    onestb = sbn("onestb", [128, 288], BF16)
    sel2   = sbn("sel2", [2, 128], F32)
    biasq  = sbn("biasq", [1, D], BF16)
    biask  = sbn("biask", [1, D], BF16)
    biasv  = sbn("biasv", [1, D], BF16)
    biaso  = sbn("biaso", [1, D], BF16)
    rbias  = sbn("rbias", [E, 1], F32)
    relsb  = sbn("relsb", [128, NRELX], BF16)
    epsln  = sbn("epsln", [128, 1], F32)
    negCt  = sbn("negC", [128, 1], F32)
    TMPS   = sbn("TMPS", [128, 16], F32)

    ARENA_BYTES = 156 * 1024
    arena = nc.alloc_sbuf_tensor("ARENA", [128, ARENA_BYTES], mybir.dt.uint8)
    AB = nc.lookup_mloc(arena).addr
    cur = [AB]
    def sba(name, shape, dtype, at=None):
        per_part = int(np.prod(shape[1:])) * ESZ[dtype]
        if at is None:
            at = cur[0]
        else:
            at = AB + at
        cur[0] = max(cur[0], at + ((per_part + 31) // 32) * 32)
        assert cur[0] <= AB + ARENA_BYTES, (name, cur[0] - AB)
        return nc.alloc_sbuf_tensor_at(name, list(shape), dtype, offset=at)

    XLF  = sba("XLF", [128, DC * T], BF16)
    HFB_AT = cur[0] - AB
    HFB  = sba("HFB", [128, HC * TB], BF16)
    LNTOK = sba("LNTOK", [128, NT * D], BF16, at=HFB_AT)
    WGC  = sba("WGC", [128, 2 * DC * 128], BF16)
    WUC  = sba("WUC", [128, 2 * DC * 128], BF16)
    WDC  = sba("WDC", [128, 2 * 512], BF16)
    WRC  = sba("WRC", [128, DC * E], BF16)
    LOGF = sba("LOGF", [E, T], F32)
    LOGT = sba("LOGT", [128, NT * E], F32)
    LWORK = sba("LWORK", [128, E], F32)
    SELW = sba("SELW", [128, E], F32)
    MSCR = sba("MSCR", [128, E], F32)
    WTOK = sba("WTOK", [128, NT * E], F32)
    WTR  = sba("WTR", [E, T], BF16)
    eall = sba("eall", [32, E * 128], BF16)
    SGT  = sba("SGT", [128, TB], BF16)
    FFN_END = cur[0] - AB
    assert FFN_END <= ARENA_BYTES, FFN_END

    XF2A = sba("XF2A", [128, DC * T], BF16, at=0)
    XB0  = sba("XB0", [128, DC * T], BF16, at=32 * 1024)
    XB1  = sba("XB1", [128, DC * T], BF16, at=64 * 1024)
    QF   = sba("QF", [128, DC * T], BF16, at=96 * 1024)
    WVC  = sba("WVC", [128, DC * 1024], BF16, at=128 * 1024)
    WQKC = sba("WQKC", [128, 2 * DC * 128], BF16, at=144 * 1024)
    KVO  = sba("KVO", [128, 2 * 1024], BF16, at=148 * 1024)
    CTXF = sba("CTXF", [128, DC * T], BF16, at=0)
    KSB  = sba("KSB", [128, 2 * L], BF16, at=32 * 1024)
    VSB  = sba("VSB", [128, NG * 128], BF16, at=48 * 1024)
    ZACC = sba("ZACC", [128, 4 * ZSLOT], F32, at=56 * 1024)
    ZQT  = sba("ZQT", [128, 2], F32, at=56 * 1024 + 256)
    ZINVR = sba("ZINVR", [2, 128], F32, at=56 * 1024 + 288)
    BIASL = sba("BIASL", [128, 4], F32, at=56 * 1024 + 1024)
    BIASR = sba("BIASR", [128, 4], F32, at=56 * 1024 + 1056)
    ZBS   = sba("ZBS", [128, 128], F32, at=56 * 1024 + 1536)
    WOC  = sba("WOC", [128, 2 * 1024], BF16, at=58 * 1024)
    PSB  = sba("PSB", [128, 2 * L], BF16, at=64 * 1024)
    PTB  = sba("PTB", [128, 2 * L], BF16, at=80 * 1024)
    QREV = sba("QREV", [128, 2 * GW], BF16, at=128 * 1024)
    GPB  = sba("GPB", [128, 4 * 2304], BF16, at=138 * 1024)
    ATT_END = 156 * 1024
    assert ATT_END <= ARENA_BYTES

    pb = [nc.alloc_psum_tensor(f"pb{i}", [128, 512], F32) for i in range(8)]

    semnames = ["ld", "lwg0", "lwg1", "lwd0", "lwd1", "lwr", "st",
                "tc", "vc", "sc", "gc", "agS",
                "pt0", "pt1", "pt2", "pt3",
                "gw0", "gw1", "sk0", "sk1",
                "kv0", "kv1", "kvv", "ks0", "ks1", "ccS"]
    sems = {n: Sem(nc, n) for n in semnames}

    def program(which):
        t = Eng(nc.tensor if which == "t" else None)
        v = Eng(nc.vector if which == "v" else None, auto_drain=True)
        s = Eng(nc.scalar if which == "s" else None, auto_drain=True)
        g = Eng(nc.gpsimd if which == "g" else None, auto_drain=True)
        y = Eng(nc.sync if which == "y" else None)
        for sm in sems.values():
            sm.n = 0
        ld, st = sems["ld"], sems["st"]
        lwg = [sems["lwg0"], sems["lwg1"]]
        lwd = [sems["lwd0"], sems["lwd1"]]
        lwr = sems["lwr"]
        agS = sems["agS"]
        ccS = sems["ccS"]
        pt = [sems["pt0"], sems["pt1"], sems["pt2"], sems["pt3"]]
        gw = [sems["gw0"], sems["gw1"]]
        sk = [sems["sk0"], sems["sk1"]]
        kvs = [sems["kv0"], sems["kv1"]]
        kvv = sems["kvv"]
        ks = [sems["ks0"], sems["ks1"]]
        tc, vc, sc, gc = sems["tc"], sems["vc"], sems["sc"], sems["gc"]

        def gdma(eng, sem, **kw):
            eng.wait(sem, sem.n)
            sem.inc(eng.dma_start(**kw), 16)

        for ins_args in [(onesr, 1.0), (onestf, 1.0), (onestb, 1.0),
                         (epsln, EPS_LN), (negCt, -C_EXP), (identb, 0.0),
                         (identf, 0.0), (sel2, 0.0), (eall, 0.0)]:
            gc.inc(g.memset(ins_args[0][:, :], ins_args[1]), 1)
        g.wait(gc, gc.n)
        gc.inc(g.affine_select(identb[:, :], onestb[:, 0:128],
                               [[-1, 128]], ALU.is_equal, 0.0,
                               base=0, channel_multiplier=1), 1)
        gc.inc(g.affine_select(identf[:, :], onestf[:, 0:128],
                               [[-1, 128]], ALU.is_equal, 0.0,
                               base=0, channel_multiplier=1), 1)
        gc.inc(g.affine_select(sel2[0:2, 0:64], onestf[0:2, 0:64], [[0, 64]],
                               ALU.is_equal, 0.0, base=0, channel_multiplier=1), 1)
        gc.inc(g.affine_select(sel2[0:2, 64:128], onestf[0:2, 0:64], [[0, 64]],
                               ALU.is_equal, 0.0, base=-1, channel_multiplier=1), 1)
        GC_INIT = gc.n

        for tt in range(NT):
            ld.inc(g.dma_start(out=X[:, tt * D:(tt + 1) * D], in_=x_in[tt]), 16)
        ld.inc(g.dma_start(out=relsb[:, :], in_=relr_d.ap()), 16)
        ld.inc(g.dma_start(out=rbias[:, :], in_=rb_d.ap()), 16)
        for bia, dsrc in [(biasq, bq_d), (biask, bk_d), (biasv, bv_d), (biaso, bo_d)]:
            ld.inc(g.dma_start(out=bia[:, :], in_=dsrc.ap()), 16)
        X_READY = ld.n

        def ln_all(wait_resid_vc, dst_f):
            s.wait(vc, wait_resid_vc)
            v.wait(vc, 0)
            for tt in range(NT):
                xs = X[:, tt * D:(tt + 1) * D]
                dst = LNTOK[:, tt * D:(tt + 1) * D]
                sc.inc(s.activation(dst, xs, AF.Square, accum_out=TMPS[:, 1:2]), 1)
                v.tensor_reduce(TMPS[:, 0:1], xs, AX.X, ALU.add, negate=True)
                v.wait(sc, sc.n)
                v.tensor_scalar(TMPS[:, 2:3], TMPS[:, 0:1], 1.0 / D, None, ALU.mult)
                v.tensor_tensor(TMPS[:, 3:4], TMPS[:, 2:3], TMPS[:, 2:3], ALU.mult)
                vc.inc(v.scalar_tensor_tensor(TMPS[:, 4:5], TMPS[:, 1:2], 1.0 / D,
                                              TMPS[:, 3:4], ALU.mult, ALU.subtract), 1)
                s.wait(vc, vc.n)
                sc.inc(s.activation(TMPS[:, 5:6], TMPS[:, 4:5], AF.Sqrt,
                                    bias=epsln[:, :]), 1)
                v.wait(sc, sc.n)
                v.reciprocal(TMPS[:, 6:7], TMPS[:, 5:6])
                vc.inc(v.tensor_scalar(dst, xs, TMPS[:, 2:3], TMPS[:, 6:7],
                                       ALU.add, ALU.mult), 1)
            LN_DONE = vc.n
            bank_free_at = [0, 0]
            for i in range(NT * DC):
                tt, c = divmod(i, DC)
                bk = pb[6 + (i % 2)]
                tdst = bk.ap().bitcast(BF16)[:, 0:128]
                if i == 0:
                    t.wait(vc, LN_DONE)
                t.wait(vc, bank_free_at[i % 2])
                tc.inc(t.transpose(tdst, LNTOK[:, tt * D + c * 128: tt * D + (c + 1) * 128],
                                   identb[:, :]), 1)
                v.wait(tc, tc.n)
                vc.inc(v.tensor_copy(dst_f[:, c * T + tt * 128: c * T + (tt + 1) * 128],
                                     tdst), 1)
                bank_free_at[i % 2] = vc.n
            return vc.n

        def ffn(call, wait_resid_vc):
            XLF_READY = ln_all(wait_resid_vc, XLF)

            gdma(g, lwr, out=WRC[:, :], in_=wr_d.ap())
            WR_LD = lwr.n
            t.wait(lwr, WR_LD)
            t.wait(vc, XLF_READY)
            for tkb in range(4):
                for dc in range(DC):
                    ins = t.matmul(pb[6][0:E, :], WRC[:, dc * E:(dc + 1) * E],
                                   XLF[:, dc * T + tkb * 512: dc * T + (tkb + 1) * 512],
                                   start=(dc == 0), stop=(dc == DC - 1))
                tc.inc(ins, 1)
                v.wait(tc, tc.n)
                vc.inc(v.tensor_scalar(LOGF[:, tkb * 512:(tkb + 1) * 512],
                                       pb[6][0:E, :], rbias[:, :], None, ALU.add), 1)
                t.wait(vc, vc.n)
            bank_free_at = [vc.n, vc.n]
            for tt in range(NT):
                bk = pb[6 + (tt % 2)]
                t.wait(vc, bank_free_at[tt % 2])
                tc.inc(t.transpose(bk.ap()[:, 0:E],
                                   LOGF[:, tt * 128:(tt + 1) * 128],
                                   identf[0:E, 0:E]), 1)
                v.wait(tc, tc.n)
                vc.inc(v.tensor_copy(LOGT[:, tt * E:(tt + 1) * E], bk.ap()[0:128, 0:E]), 1)
                bank_free_at[tt % 2] = vc.n

            for tt in range(NT):
                l0 = LOGT[:, tt * E:(tt + 1) * E]
                v.tensor_copy(LWORK[:, :], l0)
                v.memset(SELW[:, :], 0.0)
                for r in range(4):
                    v.tensor_reduce(TMPS[:, 8:9], LWORK[:, :], AX.X, ALU.max,
                                    negate=True)
                    v.tensor_scalar(MSCR[:, :], LWORK[:, :], TMPS[:, 8:9], 0.0,
                                    ALU.add, ALU.is_ge)
                    v.tensor_add(SELW[:, :], SELW[:, :], MSCR[:, :])
                    if r == 0:
                        v.tensor_copy(TMPS[:, 9:10], TMPS[:, 8:9])
                    if r < 3:
                        v.scalar_tensor_tensor(LWORK[:, :], MSCR[:, :], -30000.0,
                                               LWORK[:, :], ALU.mult, ALU.add)
                vc.inc(v.engine_nop(), 1)
                s.wait(vc, vc.n)
                sc.inc(s.activation(LWORK[:, :], l0, AF.Exp, bias=TMPS[:, 9:10]), 1)
                v.wait(sc, sc.n)
                v.tensor_mul(LWORK[:, :], LWORK[:, :], SELW[:, :])
                v.tensor_reduce(TMPS[:, 10:11], LWORK[:, :], AX.X, ALU.add)
                v.reciprocal(TMPS[:, 11:12], TMPS[:, 10:11])
                vc.inc(v.tensor_scalar(WTOK[:, tt * E:(tt + 1) * E], LWORK[:, :],
                                       TMPS[:, 11:12], None, ALU.mult), 1)
            bank_free_at = [vc.n, vc.n]
            for tt in range(NT):
                bk = pb[6 + (tt % 2)]
                t.wait(vc, bank_free_at[tt % 2])
                tc.inc(t.transpose(bk.ap()[0:E, 0:128],
                                   WTOK[:, tt * E:(tt + 1) * E], identf[:, :]), 1)
                v.wait(tc, tc.n)
                vc.inc(v.tensor_copy(WTR[:, tt * 128:(tt + 1) * 128],
                                     bk.ap()[0:E, 0:128]), 1)
                bank_free_at[tt % 2] = vc.n
            WTR_READY = vc.n

            g.wait(vc, wait_resid_vc)
            gc.inc(g.memset(eall[:, :], 0.0), 1)
            g.wait(gc, gc.n)
            for e in range(E):
                gc.inc(g.affine_select(eall[:, e * 128:(e + 1) * 128],
                                       onestb[0:32, 0:128], [[0, 128]],
                                       ALU.is_equal, 0.0,
                                       base=-e, channel_multiplier=1), 1)
            EALL_RDY = gc.n

            step_consumed = {}
            down_evac_end = vc.n
            for tb in range(NTB):
                tbo = tb * TB
                for hc in range(HC):
                    wpar = hc % 2
                    it = tb * HC + hc
                    if it >= 2:
                        g.wait(vc, step_consumed[2 * (it - 2) + 1])
                    g.wait(lwg[wpar], lwg[wpar].n)
                    lwg[wpar].inc(g.dma_start(out=WGC[:, wpar * 1024:(wpar + 1) * 1024],
                                       in_=wg_d[hc]), 16)
                    lwg[wpar].inc(g.dma_start(out=WUC[:, wpar * 1024:(wpar + 1) * 1024],
                                       in_=wu_d[hc]), 16)
                    W_LD = lwg[wpar].n
                    t.wait(lwg[wpar], W_LD)
                    if it == 0:
                        t.wait(gc, EALL_RDY)
                        t.wait(vc, WTR_READY)
                        t.wait(vc, down_evac_end)
                    for th in range(2):
                        stp = 2 * it + th
                        sl = stp % 2
                        gbank, ubank, wbank = pb[0 + 2 * sl], pb[1 + 2 * sl], pb[4 + sl]
                        if stp >= 2:
                            t.wait(vc, step_consumed[stp - 2])
                        mvx = XLF
                        co = tbo + th * 512
                        if hc < E:
                            t.matmul(wbank.ap(), eall[:, hc * 128:(hc + 1) * 128],
                                     WTR[0:E, co:co + 512], start=True, stop=True)
                        for dc in range(DC):
                            t.matmul(gbank.ap(),
                                     WGC[:, wpar * 1024 + dc * 128: wpar * 1024 + (dc + 1) * 128],
                                     mvx[:, dc * T + co: dc * T + co + 512],
                                     start=(dc == 0), stop=(dc == DC - 1))
                        glast = None
                        for dc in range(DC):
                            glast = t.matmul(ubank.ap(),
                                     WUC[:, wpar * 1024 + dc * 128: wpar * 1024 + (dc + 1) * 128],
                                     mvx[:, dc * T + co: dc * T + co + 512],
                                     start=(dc == 0), stop=(dc == DC - 1))
                        tc.inc(glast, 1)
                        GU_DONE = tc.n
                        s.wait(tc, GU_DONE)
                        sc.inc(s.activation(SGT[:, th * 512:(th + 1) * 512],
                                            gbank.ap(), AF.Silu), 1)
                        v.wait(sc, sc.n)
                        hss = HFB[:, hc * TB + th * 512: hc * TB + (th + 1) * 512]
                        v.scalar_tensor_tensor(hss, ubank.ap(), 1.0,
                                               SGT[:, th * 512:(th + 1) * 512],
                                               ALU.mult, ALU.mult)
                        if hc < E:
                            vins = v.tensor_tensor(hss, hss, wbank.ap(), ALU.mult)
                        else:
                            vins = v.engine_nop()
                        vc.inc(vins, 1)
                        step_consumed[stp] = vc.n
                H_DONE = vc.n

                wd_consumed = {}
                for half in range(2):
                    for hc in range(HC):
                        wpar = hc % 2
                        ih = half * HC + hc
                        if ih >= 2:
                            g.wait(tc, wd_consumed[ih - 2])
                        gdma(g, lwd[wpar], out=WDC[:, wpar * 512:(wpar + 1) * 512],
                             in_=wd_d[half, hc])
                        WD_LD = lwd[wpar].n
                        t.wait(lwd[wpar], WD_LD)
                        if hc == 0:
                            t.wait(vc, H_DONE if half == 0 else vc.n)
                        ins = None
                        for tt8 in range(8):
                            ins = t.matmul(pb[tt8].ap(),
                                           HFB[:, hc * TB + tt8 * 128:
                                               hc * TB + (tt8 + 1) * 128],
                                           WDC[:, wpar * 512:(wpar + 1) * 512],
                                           start=(hc == 0), stop=(hc == HC - 1))
                        tc.inc(ins, 1)
                        wd_consumed[ih] = tc.n
                    DOWN_DONE = tc.n
                    v.wait(tc, DOWN_DONE)
                    for tt8 in range(8):
                        gt = tb * 8 + tt8
                        xsl = X[:, gt * D + half * 512: gt * D + (half + 1) * 512]
                        vc.inc(v.scalar_tensor_tensor(xsl, pb[tt8].ap(), 0.5,
                                                      xsl, ALU.mult, ALU.add), 1)
                    down_evac_end = vc.n
                    t.wait(vc, down_evac_end)
            return vc.n

        def attention(wait_resid_vc):
            XF2_READY = ln_all(wait_resid_vc, XF2A)

            g.wait(vc, XF2_READY)
            agS.inc(g.dma_start(out=ag_in.ap(), in_=XF2A[:, :]), 16)
            g.wait(agS, agS.n)
            ccS.inc(g.collective_compute(
                "AllGather", ALU.bypass,
                replica_groups=[[0, 1], [2, 3], [4, 5], [6, 7]],
                ins=[ag_in.ap().opt()], outs=[ag_out.ap().opt()]), 1)
            g.wait(ccS, ccS.n)
            agS.inc(g.dma_start(out=XB0[:, :], in_=ag_out[0]), 16)
            agS.inc(g.dma_start(out=XB1[:, :], in_=ag_out[1]), 16)
            XF2B_READY = agS.n

            t.wait(agS, XF2B_READY)
            t.wait(vc, XF2_READY)
            gdma(g, lwr,
                 out=bass.AP(WVC, 0, [[DC * 1024, 128], [1024, DC], [1, 1024]]),
                 in_=bass.AP(wv_d, 0, [[1024, 128], [131072, DC], [1, 1024]]))
            WVC_LD = lwr.n
            qk_cons = {}
            for qc in range(DC):
                wpar = qc % 2
                if qc >= 2:
                    g.wait(tc, qk_cons[qc - 2])
                gdma(g, lwg[wpar],
                     out=WQKC[:, wpar * 1024:(wpar + 1) * 1024], in_=wq_d[qc])
                t.wait(lwg[wpar], lwg[wpar].n)
                for tkb in range(4):
                    bk = pb[tkb % 4]
                    for dc in range(DC):
                        t.matmul(bk.ap(),
                                 WQKC[:, wpar * 1024 + dc * 128: wpar * 1024 + (dc + 1) * 128],
                                 XF2A[:, dc * T + tkb * 512: dc * T + (tkb + 1) * 512],
                                 start=(dc == 0), stop=False)
                    tc.inc(t.matmul(bk.ap(), biasq[0:1, qc * 128:(qc + 1) * 128],
                                    onesr[0:1, 0:512], start=False, stop=True), 1)
                    v.wait(tc, tc.n)
                    vc.inc(v.tensor_copy(
                        QF[:, qc * T + tkb * 512: qc * T + (tkb + 1) * 512], bk.ap()), 1)
                    t.wait(vc, vc.n)
                qk_cons[qc] = tc.n
            QF_READY = vc.n
            kv_ev = 0
            for qc in range(DC):
                wpar = qc % 2
                g.wait(tc, qk_cons[DC + qc - 2] if qc >= 2 else qk_cons[DC - 1])
                gdma(g, lwg[wpar],
                     out=WQKC[:, wpar * 1024:(wpar + 1) * 1024], in_=wk_d[qc])
                t.wait(lwg[wpar], lwg[wpar].n)
                for half in range(2):
                    xsrc = XB0 if half == 0 else XB1
                    for grp in range(4):
                        bk = pb[(half * 4 + grp) % 4]
                        for dc in range(DC):
                            t.matmul(bk.ap(),
                                     WQKC[:, wpar * 1024 + dc * 128: wpar * 1024 + (dc + 1) * 128],
                                     xsrc[:, dc * T + grp * 512: dc * T + (grp + 1) * 512],
                                     start=(dc == 0), stop=False)
                        tc.inc(t.matmul(bk.ap(), biask[0:1, qc * 128:(qc + 1) * 128],
                                        onesr[0:1, 0:512], start=False, stop=True), 1)
                        kb = kv_ev % 2
                        v.wait(tc, tc.n)
                        v.wait(ks[kb], ks[kb].n)
                        vc.inc(v.tensor_scalar(KVO[:, kb * 1024: kb * 1024 + 512],
                                               bk.ap(), 0.125, None, ALU.mult), 1)
                        t.wait(vc, vc.n)
                        g.wait(vc, vc.n)
                        off = qc * 524288 + grp * 1024 + half * 128
                        gdma(g, ks[kb],
                             out=bass.AP(k_hbm, off, [[4096, 128], [256, 4], [1, 128]]),
                             in_=bass.AP(KVO, kb * 1024,
                                         [[2048, 128], [128, 4], [1, 128]]))
                        kv_ev += 1
                qk_cons[DC + qc] = tc.n
            t.wait(lwr, WVC_LD)
            for half in range(2):
                xsrc = XB0 if half == 0 else XB1
                for lt in range(NT):
                    bks = [pb[(2 * lt) % 4], pb[(2 * lt) % 4 + 1]]
                    for vh in range(2):
                        bk = bks[vh]
                        for dc in range(DC):
                            t.matmul(bk.ap(),
                                     xsrc[:, dc * T + lt * 128: dc * T + (lt + 1) * 128],
                                     WVC[:, dc * 1024 + vh * 512: dc * 1024 + (vh + 1) * 512],
                                     start=(dc == 0), stop=False)
                        tc.inc(t.matmul(bk.ap(), onesr[0:1, 0:128],
                                        biasv[0:1, vh * 512:(vh + 1) * 512],
                                        start=False, stop=True), 1)
                    kb = kv_ev % 2
                    v.wait(tc, tc.n)
                    v.wait(ks[kb], ks[kb].n)
                    v.tensor_copy(KVO[:, kb * 1024: kb * 1024 + 512], bks[0].ap())
                    vc.inc(v.tensor_copy(KVO[:, kb * 1024 + 512: kb * 1024 + 1024],
                                         bks[1].ap()), 1)
                    t.wait(vc, vc.n)
                    g.wait(vc, vc.n)
                    gdma(g, ks[kb],
                         out=v_hbm[2 * lt + half],
                         in_=KVO[:, kb * 1024: kb * 1024 + 1024])
                    kv_ev += 1
            KS_FIN = [ks[0].n, ks[1].n]
            QKV_T_DONE = tc.n

            g.wait(ks[0], KS_FIN[0])
            g.wait(ks[1], KS_FIN[1])

            t.wait(vc, QF_READY)
            C_QREL = [(c * 512, min(NRELX, (c + 1) * 512)) for c in range(5)]
            st_qrel_vc = 0
            st_gw = 0
            st_relmm = {0: 0, 1: 0}
            st_ctx_free = {0: 0, 1: 0}
            st_pb5_free = 0
            st_last_pv = 0
            st_trp_free = {0: 0, 1: 0}
            st_psb_free = {0: 0, 1: 0}
            st_ptb_free = 0
            ksb_free = {}
            vsb_free = {}

            for hp in range(8):
                kb = hp % 2
                if hp >= 2:
                    g.wait(tc, ksb_free[hp - 2])
                gdma(g, kvs[kb], out=KSB[:, kb * L:(kb + 1) * L], in_=k_hbm[hp])
                KSB_LD = kvs[kb].n
                if hp >= 1:
                    g.wait(tc, vsb_free[hp - 1])
                gdma(g, kvv,
                     out=bass.AP(VSB, 0, [[NG * 128, 128], [128, NG], [1, 128]]),
                     in_=bass.AP(v_hbm, hp * 128,
                                 [[1024, 128], [131072, NG], [1, 128]]))
                VSB_LD = kvv.n

                qrel_ev = {}
                sk_mark = {}
                exp_mark = {}
                pv_last = {}
                ctx_ev = {}
                ptb_ready = {}

                def emit_qrel(lt):
                    nonlocal st_qrel_vc, st_gw
                    qbuf = lt % 2
                    jlo, jhi = win(lt)
                    v.wait(gw[0], st_gw)
                    for hip in range(2):
                        for c0, c1 in C_QREL:
                            t.wait(vc, st_qrel_vc)
                            tc.inc(t.matmul(
                                pb[4][:, 0:c1 - c0],
                                QF[hip * 64:(hip + 1) * 64,
                                   hp * T + lt * 128: hp * T + (lt + 1) * 128],
                                relsb[hip * 64:(hip + 1) * 64, c0:c1],
                                start=True, stop=True,
                                tile_position=(hip * 64, 0)), 1)
                            v.wait(tc, tc.n)
                            st_qrel_vc = vc.inc(v.tensor_copy(
                                QREV[:, hip * GW + PAD + c0: hip * GW + PAD + c1],
                                pb[4][:, 0:c1 - c0]), 1)
                        v.tensor_copy(TMPS[:, 12:13],
                                      QREV[:, hip * GW + PAD: hip * GW + PAD + 1])
                        v.tensor_copy(TMPS[:, 13:14],
                                      QREV[:, hip * GW + PAD + NRELX - 1: hip * GW + PAD + NRELX])
                        v.tensor_scalar(QREV[:, hip * GW: hip * GW + PAD],
                                        onestb[:, 0:PAD],
                                        TMPS[:, 12:13], None, ALU.mult)
                        v.tensor_scalar(
                            QREV[:, hip * GW + PAD + NRELX: hip * GW + GW],
                            onestb[:, 0:GW - PAD - NRELX],
                            TMPS[:, 13:14], None, ALU.mult)
                        v.tensor_scalar(BIASL[:, qbuf * 2 + hip: qbuf * 2 + hip + 1],
                                        QREV[:, hip * GW + PAD: hip * GW + PAD + 1],
                                        -C_EXP, None, ALU.add)
                        v.tensor_scalar(BIASR[:, qbuf * 2 + hip: qbuf * 2 + hip + 1],
                                        QREV[:, hip * GW + PAD + NRELX - 1: hip * GW + PAD + NRELX],
                                        -C_EXP, None, ALU.add)
                    qrel_ev[lt] = vc.inc(v.engine_nop(), 1)
                    g.wait(vc, qrel_ev[lt])
                    g.wait(sk[qbuf], sk[qbuf].n)
                    st_gw = gw[0].inc(g.dma_start(out=g_hbm[qbuf], in_=QREV[:, :]), 16)

                def emit_skew(lt):
                    qbuf = lt % 2
                    jlo, jhi = win(lt)
                    g.wait(gw[0], st_gw)
                    g.wait(tc, st_relmm[qbuf])
                    i0c = 256 * lt
                    W = jhi - jlo
                    g.wait(sk[qbuf], sk[qbuf].n)
                    for hip in range(2):
                        base = qbuf * 128 * (2 * GW) + hip * GW + (1278 - i0c + jlo)
                        sk[qbuf].inc(g.dma_start(
                            out=GPB[:, (qbuf * 2 + hip) * 2304: (qbuf * 2 + hip) * 2304 + W],
                            in_=bass.AP(g_hbm, base,
                                        [[2 * GW - 1, 128], [1, W]])), 16)
                    sk_mark[lt] = sk[qbuf].n

                def emit_pv_chunk(lq, kc):
                    cbk = pb[6 + lq % 2]
                    if kc == 0:
                        t.wait(vc, ptb_ready[lq])
                        t.wait(kvv, VSB_LD)
                        t.wait(vc, st_ctx_free[lq % 2])
                    t.matmul(cbk[0:64, 0:128],
                             VSB[:, kc * 128: kc * 128 + 64],
                             PTB[:, 0 * L + kc * 128: 0 * L + (kc + 1) * 128],
                             start=(kc == 0), stop=(kc == NG - 1),
                             tile_position=(0, 0), skip_group_check=True)
                    lastm = t.matmul(cbk[64:128, 0:128],
                             VSB[:, kc * 128 + 64: (kc + 1) * 128],
                             PTB[:, 1 * L + kc * 128: 1 * L + (kc + 1) * 128],
                             start=(kc == 0), stop=(kc == NG - 1),
                             tile_position=(0, 64), skip_group_check=True)
                    if kc == NG - 1:
                        pv_last[lq] = tc.inc(lastm, 1)
                        st_last_pv_set(pv_last[lq])

                def st_last_pv_set(vv):
                    nonlocal st_last_pv
                    st_last_pv = vv

                def emit_z(lq):
                    nonlocal st_pb5_free
                    qb = lq % 2
                    cbk = pb[6 + qb]
                    v.wait(sc, exp_all_m[(1, lq)])
                    v.tensor_reduce(ZQT[:, 0:1],
                                    ZACC[:, qb * 2 * ZSLOT: qb * 2 * ZSLOT + ZSLOT],
                                    AX.X, ALU.add)
                    v.tensor_reduce(ZQT[:, 1:2],
                                    ZACC[:, qb * 2 * ZSLOT + ZSLOT: (qb + 1) * 2 * ZSLOT],
                                    AX.X, ALU.add)
                    zr = vc.inc(v.tensor_scalar(ZQT[:, :], ZQT[:, :],
                                                1.0 + 1e-8, None, ALU.mult), 1)
                    t.wait(vc, zr)
                    t.wait(vc, st_pb5_free)
                    zt = tc.inc(t.transpose(pb[5][0:2, 0:128], ZQT[:, :],
                                            identf[:, :]), 1)
                    v.wait(tc, zt)
                    rcp = vc.inc(v.reciprocal(ZINVR[0:2, :], pb[5][0:2, 0:128]), 1)
                    t.wait(vc, rcp)
                    zb = tc.inc(t.matmul(pb[5][:, 128:256], sel2[0:2, :],
                                         ZINVR[0:2, :], start=True, stop=True), 1)
                    v.wait(tc, zb)
                    v.wait(tc, pv_last[lq])
                    v.tensor_copy(ZBS[:, :], pb[5][:, 128:256])
                    ctx_ev[lq] = vc.inc(v.tensor_tensor(
                        CTXF[:, hp * T + lq * 128: hp * T + (lq + 1) * 128],
                        cbk[:, 0:128], ZBS[:, :], ALU.mult), 1)
                    st_ctx_free[qb] = ctx_ev[lq]
                    st_pb5_free = ctx_ev[lq]

                emit_qrel(0)
                emit_skew(0)
                emit_qrel(1)
                emit_skew(1)
                exp_all_m = {}

                for qt in range(NT):
                    qbuf = qt % 2
                    jlo, jhi = win(qt)
                    if qt + 2 < NT:
                        emit_qrel(qt + 2)
                    sc0 = sc.n
                    zm = vc.inc(v.memset(ZACC[:, qbuf * 2 * ZSLOT:
                                              (qbuf + 1) * 2 * ZSLOT], 0.0), 1)
                    for kc in range(8):
                        c0, c1 = kc * 512, (kc + 1) * 512
                        b0, b1 = max(c0, jlo), min(c1, jhi)
                        smm = {}
                        t.wait(sc, sc0 + 2 * kc)
                        if kc == 0:
                            t.wait(kvs[kb], KSB_LD)
                            t.wait(sk[qbuf], sk_mark[qt])
                        for hip in range(2):
                            smm[hip] = t.matmul(
                                pb[hip].ap(),
                                QF[hip * 64:(hip + 1) * 64,
                                   hp * T + qt * 128: hp * T + (qt + 1) * 128],
                                KSB[hip * 64:(hip + 1) * 64, kb * L + c0: kb * L + c1],
                                start=True, stop=(b0 >= b1),
                                tile_position=(hip * 64, 0))
                        for hip in range(2):
                            lastm = smm[hip]
                            if b0 < b1:
                                lastm = t.matmul(
                                    pb[hip][:, b0 - c0: b1 - c0],
                                    identb[:, :],
                                    GPB[:, (qbuf * 2 + hip) * 2304 + (b0 - jlo):
                                        (qbuf * 2 + hip) * 2304 + (b1 - jlo)],
                                    start=False, stop=True)
                            exp_mark[(hip, qt, kc)] = tc.inc(lastm, 1)
                        if qt >= 1:
                            for j in range(4):
                                emit_pv_chunk(qt - 1, 4 * kc + j)
                    st_relmm[qbuf] = tc.n

                    s.wait(vc, zm)
                    s.wait(vc, qrel_ev[qt])
                    s.wait(tc, st_psb_free[0])
                    s.wait(tc, st_psb_free[1])
                    slot = [0, 0]
                    for kc in range(8):
                        c0, c1 = kc * 512, (kc + 1) * 512
                        for hip in range(2):
                            bkS = pb[hip]
                            s.wait(tc, exp_mark[(hip, qt, kc)])
                            segs = []
                            if c0 < jlo:
                                segs.append((c0, min(c1, jlo),
                                             BIASL[:, qbuf * 2 + hip: qbuf * 2 + hip + 1]))
                            if max(c0, jlo) < min(c1, jhi):
                                segs.append((max(c0, jlo), min(c1, jhi), negCt[:, :]))
                            if jhi < c1:
                                segs.append((max(c0, jhi), c1,
                                             BIASR[:, qbuf * 2 + hip: qbuf * 2 + hip + 1]))
                            last = None
                            for (s0, s1, bias_ap) in segs:
                                last = s.activation(
                                    PSB[:, hip * L + s0: hip * L + s1],
                                    bkS[:, s0 - c0: s1 - c0],
                                    AF.Exp, bias=bias_ap,
                                    accum_out=ZACC[:, qbuf * 2 * ZSLOT + hip * ZSLOT + slot[hip]:
                                                   qbuf * 2 * ZSLOT + hip * ZSLOT + slot[hip] + 1])
                                slot[hip] += 1
                            sc.inc(last, 1)
                    exp_all_m[(0, qt)] = sc.n
                    exp_all_m[(1, qt)] = sc.n

                    if qt >= 1:
                        emit_z(qt - 1)

                    for hip in range(2):
                        for j4 in range(NG // 4):
                            bkT = pb[2 + j4 % 2]
                            t.wait(vc, st_trp_free[j4 % 2])
                            t.wait(sc, sc0 + 2 * j4 + hip + 1)
                            if hip == 0 and j4 == 0:
                                t.wait(tc, pv_last[qt - 1] if qt >= 1 else st_ptb_free)
                            lastm = None
                            for jj in range(4):
                                j = j4 * 4 + jj
                                lastm = t.transpose(
                                    bkT.ap().bitcast(BF16)[:, jj * 128:(jj + 1) * 128],
                                    PSB[:, hip * L + j * 128: hip * L + (j + 1) * 128],
                                    identb[:, :])
                            tc.inc(lastm, 1)
                            if j4 == NG // 4 - 1:
                                st_psb_free[hip] = tc.n
                            v.wait(tc, tc.n)
                            st_trp_free[j4 % 2] = vc.inc(v.tensor_copy(
                                PTB[:, hip * L + j4 * 512: hip * L + (j4 + 1) * 512],
                                bkT.ap().bitcast(BF16)[:, 0:512]), 1)
                    ptb_ready[qt] = vc.n
                    if qt + 2 < NT:
                        emit_skew(qt + 2)

                for kc in range(NG):
                    emit_pv_chunk(NT - 1, kc)
                emit_z(NT - 1)
                st_ptb_free = pv_last[NT - 1]
                ksb_free[hp] = tc.n
                vsb_free[hp] = tc.n

            op_bank_free = {}
            for grp in range(2):
                for half in range(2):
                    for dc in range(DC):
                        wpar = dc % 2
                        gi = (grp * 2 + half) * DC + dc
                        g.wait(tc, op_bank_free.get(gi - 2, 0))
                        gdma(g, lwg[wpar],
                             out=WOC[:, wpar * 1024: wpar * 1024 + 512],
                             in_=wo_d.ap()[dc * 128:(dc + 1) * 128,
                                           half * 512:(half + 1) * 512])
                        t.wait(lwg[wpar], lwg[wpar].n)
                        if dc == 0:
                            t.wait(vc, vc.n)
                        for q8 in range(8):
                            qt = grp * 8 + q8
                            lastm = t.matmul(
                                pb[q8].ap(),
                                CTXF[:, dc * T + qt * 128: dc * T + (qt + 1) * 128],
                                WOC[:, wpar * 1024: wpar * 1024 + 512],
                                start=(dc == 0), stop=False)
                        if dc == DC - 1:
                            for q8 in range(8):
                                lastm = t.matmul(pb[q8].ap(), onesr[0:1, 0:128],
                                                 biaso[0:1, half * 512:(half + 1) * 512],
                                                 start=False, stop=True)
                        tc.inc(lastm, 1)
                        op_bank_free[gi] = tc.n
                    v.wait(tc, tc.n)
                    for q8 in range(8):
                        qt = grp * 8 + q8
                        xsl = X[:, qt * D + half * 512: qt * D + (half + 1) * 512]
                        vc.inc(v.tensor_tensor(xsl, xsl, pb[q8].ap(), ALU.add), 1)
                    t.wait(vc, vc.n)
            return vc.n

        t.wait(gc, GC_INIT); v.wait(gc, GC_INIT); s.wait(gc, GC_INIT); y.wait(gc, GC_INIT)
        t.wait(ld, X_READY); v.wait(ld, X_READY); s.wait(ld, X_READY)
        resid_vc = 0
        resid_vc = ffn(0, resid_vc)
        resid_vc = attention(resid_vc)
        resid_vc = ffn(1, resid_vc)

        g.wait(vc, resid_vc)
        g.wait(tc, sems["tc"].n)
        for tt in range(NT):
            st.inc(g.dma_start(out=out_d[tt], in_=X[:, tt * D:(tt + 1) * D]), 16)
        g.wait(st, st.n)

    with nc.Block() as blk:
        blk.tensor(lambda e: program("t"))
        blk.vector(lambda e: program("v"))
        blk.scalar(lambda e: program("s"))
        blk.gpsimd(lambda e: program("g"))
        blk.sync(lambda e: program("y"))
    return nc


def _prep_weights(inputs):
    f32 = np.float32
    wq = np.ascontiguousarray(inputs["L0_attn_Wq"], dtype=f32)
    wk = np.ascontiguousarray(inputs["L0_attn_Wk"], dtype=f32)

    def qk_layout(w):
        w4 = w.reshape(DC, 128, DC, 128)
        return np.ascontiguousarray(w4.transpose(2, 1, 0, 3).reshape(DC, 128, DC * 128))

    gate = np.asarray(inputs["L0_ffn_gate_w"], dtype=f32).reshape(4096, D)
    up = np.asarray(inputs["L0_ffn_up_w"], dtype=f32).reshape(4096, D)
    sgate = np.asarray(inputs["L0_ffn_sgate_w"], dtype=f32)
    sup = np.asarray(inputs["L0_ffn_sup_w"], dtype=f32)
    down = np.asarray(inputs["L0_ffn_down_w"], dtype=f32).transpose(0, 2, 1).reshape(4096, D)
    sdown = np.asarray(inputs["L0_ffn_sdown_w"], dtype=f32).T

    gall = np.concatenate([gate, sgate], 0)
    uall = np.concatenate([up, sup], 0)
    dall = np.concatenate([down, sdown], 0)

    def gu_layout(w):
        w4 = w.reshape(HC, 128, DC, 128)
        w4 = w.reshape(HC, 128, DC, 128)
        return np.ascontiguousarray(w4.transpose(0, 3, 2, 1).reshape(HC, 128, DC * 128))

    def down_layout(w):
        w4 = w.reshape(HC, 128, 2, 512)
        return np.ascontiguousarray(w4.transpose(2, 0, 1, 3))

    router = np.asarray(inputs["L0_ffn_router_w"], dtype=f32)
    wr = np.ascontiguousarray(router.reshape(E, DC, 128).transpose(2, 1, 0)
                              .reshape(128, DC * E))

    rel = np.asarray(inputs["L0_attn_rel_emb"], dtype=f32)
    rel_rev = rel[::-1]
    relx = {}
    for par in (0, 1):
        idx = np.clip(np.arange(NRELX) - 128 * par, 0, NREL - 1)
        rx = rel_rev[idx].T
        relx[par] = np.ascontiguousarray(np.concatenate([rx, rx], axis=0))

    out = {
        "wq": qk_layout(wq), "wk": qk_layout(wk),
        "wv": np.ascontiguousarray(inputs["L0_attn_Wv"], dtype=f32),
        "wo": np.ascontiguousarray(inputs["L0_attn_Wo"], dtype=f32),
        "bq": np.ascontiguousarray(inputs["L0_attn_bq"], dtype=f32).reshape(1, D),
        "bk": np.ascontiguousarray(inputs["L0_attn_bk"], dtype=f32).reshape(1, D),
        "bv": np.ascontiguousarray(inputs["L0_attn_bv"], dtype=f32).reshape(1, D),
        "bo": np.ascontiguousarray(inputs["L0_attn_bo"], dtype=f32).reshape(1, D),
        "wr": wr,
        "rb": np.ascontiguousarray(inputs["L0_ffn_router_bias"], dtype=f32).reshape(E, 1),
        "wg": gu_layout(gall), "wu": gu_layout(uall), "wd": down_layout(dall),
    }
    return out, relx


_NC_CACHE = {}

def kernel(**inputs):
    if "prog" not in _NC_CACHE:
        _NC_CACHE["prog"] = build_program()
    nc = _NC_CACHE["prog"]
    w, relx = _prep_weights(inputs)
    x = np.asarray(inputs["L0_attn_x"], dtype=np.float32)
    in_maps = []
    for c in range(8):
        b, par = c // 2, c % 2
        xt = np.ascontiguousarray(
            x[b].reshape(NG, 128, D)[par::2])
        im = dict(w)
        im["x"] = xt
        im["relr"] = relx[par]
        in_maps.append(im)
    res = run_bass_kernel_spmd(nc, in_maps, core_ids=list(range(8)))
    out = np.zeros((B, NG, 128, D), dtype=np.float32)
    for c in range(8):
        b, par = c // 2, c % 2
        out[b, par::2] = res.results[c]["out"]
    return out.reshape(B, L, D)


# version 11
# speedup vs baseline: 1.0245x; 1.0245x over previous
import os
import numpy as np
import ml_dtypes
import concourse.bass as bass
import concourse.mybir as mybir
from concourse.bass_utils import run_bass_kernel_spmd
import concourse.bass_utils as _bu

_orig_run_command = _bu.run_command
def _run_command_ldw(argv, **kw):
    argv = ["--enable-ldw-opt=true" if a == "--enable-ldw-opt=false" else a
            for a in argv]
    return _orig_run_command(argv, **kw)
_bu.run_command = _run_command_ldw

F32 = mybir.dt.float32
BF16 = mybir.dt.bfloat16
U32 = mybir.dt.uint32
AF = mybir.ActivationFunctionType
ALU = mybir.AluOpType
AX = mybir.AxisListType
bfdt = ml_dtypes.bfloat16

B, L, D = 4, 4096, 1024
H, DH = 16, 64
E = 32
HS = 512
T = 2048
NT = 16
NG = 32
DC = 8
HC = 36
TB = 1024
NTB = T // TB
C_EXP = 12.0
NREL = 2047
NRELX = 2175
PAD = 255
GW = 2560
EPS_LN = 1e-5
ZSLOT = 12

DEBUG_STAGE = 1


def win(lt):
    i0t = 256 * lt
    return max(0, i0t - 1023), min(L, i0t + 1279)


class Sem:
    def __init__(self, nc, name):
        self.h = nc.semaphore(name).__enter__()
        self.n = 0
    def inc(self, instr, k=1):
        self.n += k
        if instr is not None:
            instr.then_inc(self.h, k)
        return self.n


_DRAIN_OPS = {
    "memset", "memzero", "tensor_copy", "tensor_tensor", "tensor_add",
    "tensor_sub", "tensor_mul", "tensor_max", "tensor_scalar",
    "tensor_single_scalar", "tensor_scalar_add", "tensor_scalar_sub",
    "tensor_scalar_mul", "tensor_scalar_max", "tensor_scalar_min",
    "tensor_relu", "tensor_reduce", "reduce_max", "reduce_sum",
    "scalar_tensor_tensor", "select", "reciprocal", "activation", "square",
    "sqrt", "mul", "copy", "add", "sign", "affine_select", "iota",
    "bn_stats", "bn_aggr", "transpose", "stream_shuffle",
}


class Eng:
    def __init__(self, real, auto_drain=False):
        object.__setattr__(self, "real", real)
        object.__setattr__(self, "auto_drain", auto_drain)
        object.__setattr__(self, "_last_was_op", False)
    def __getattr__(self, name):
        real = object.__getattribute__(self, "real")
        if real is None:
            return lambda *a, **k: None
        fn = getattr(real, name)
        if name in _DRAIN_OPS and object.__getattribute__(self, "auto_drain"):
            def wrapped(*a, **k):
                if object.__getattribute__(self, "_last_was_op"):
                    real.drain()
                object.__setattr__(self, "_last_was_op", True)
                return fn(*a, **k)
            return wrapped
        return fn
    def wait(self, sem, val):
        real = object.__getattribute__(self, "real")
        if real is not None and val > 0:
            real.wait_ge(sem.h, val)


def build_program():
    nc = bass.Bass("TRN2", target_bir_lowering=False,
                   detect_race_conditions=os.environ.get("KRACE", "0") == "1")

    def din(name, shape, dtype=F32):
        return nc.dram_tensor(name, shape, dtype, kind="ExternalInput")

    x_in = din("x", [NT, 128, D])
    wq_d = din("wq", [DC, 128, DC * 128])
    wk_d = din("wk", [DC, 128, DC * 128])
    wv_d = din("wv", [D, D])
    wo_d = din("wo", [D, D])
    bq_d = din("bq", [1, D]); bk_d = din("bk", [1, D])
    bv_d = din("bv", [1, D]); bo_d = din("bo", [1, D])
    relr_d = din("relr", [128, NRELX])
    wr_d = din("wr", [128, DC * E])
    rb_d = din("rb", [E, 1])
    wg_d = din("wg", [HC, 128, DC * 128])
    wu_d = din("wu", [HC, 128, DC * 128])
    wd_d = din("wd", [2, HC, 128, 512])

    out_d = nc.dram_tensor("out", [NT, 128, D], F32, kind="ExternalOutput")

    k_hbm = nc.dram_tensor("k_hbm", [DC, 128, L], BF16, kind="Internal")
    v_hbm = nc.dram_tensor("v_hbm", [NG, 128, D], BF16, kind="Internal")
    g_hbm = nc.dram_tensor("g_hbm", [2, 128, 2 * GW], BF16, kind="Internal")
    ag_in = nc.dram_tensor("ag_in", [128, DC * T], BF16, kind="Internal")
    ag_out = nc.dram_tensor("ag_out", [2, 128, DC * T], BF16, kind="Internal")

    ESZ = {F32: 4, BF16: 2, U32: 4}

    def sbn(name, shape, dtype):
        return nc.alloc_sbuf_tensor(name, list(shape), dtype)

    X      = sbn("X", [128, NT * D], BF16)
    identb = sbn("identb", [128, 128], BF16)
    identf = sbn("identf", [128, 128], F32)
    onesr  = sbn("onesr", [1, 512], BF16)
    onestf = sbn("onestf", [128, 256], F32)
    onestb = sbn("onestb", [128, 288], BF16)
    sel2   = sbn("sel2", [2, 128], F32)
    biasq  = sbn("biasq", [1, D], BF16)
    biask  = sbn("biask", [1, D], BF16)
    biasv  = sbn("biasv", [1, D], BF16)
    biaso  = sbn("biaso", [1, D], BF16)
    rbias  = sbn("rbias", [E, 1], F32)
    relsb  = sbn("relsb", [128, NRELX], BF16)
    epsln  = sbn("epsln", [128, 1], F32)
    negCt  = sbn("negC", [128, 1], F32)
    TMPS   = sbn("TMPS", [128, 16], F32)

    ARENA_BYTES = 156 * 1024
    arena = nc.alloc_sbuf_tensor("ARENA", [128, ARENA_BYTES], mybir.dt.uint8)
    AB = nc.lookup_mloc(arena).addr
    cur = [AB]
    def sba(name, shape, dtype, at=None):
        per_part = int(np.prod(shape[1:])) * ESZ[dtype]
        if at is None:
            at = cur[0]
        else:
            at = AB + at
        cur[0] = max(cur[0], at + ((per_part + 31) // 32) * 32)
        assert cur[0] <= AB + ARENA_BYTES, (name, cur[0] - AB)
        return nc.alloc_sbuf_tensor_at(name, list(shape), dtype, offset=at)

    XLF  = sba("XLF", [128, DC * T], BF16)
    HFB_AT = cur[0] - AB
    HFB  = sba("HFB", [128, HC * TB], BF16)
    LNTOK = sba("LNTOK", [128, NT * D], BF16, at=HFB_AT)
    WGC  = sba("WGC", [128, 2 * DC * 128], BF16)
    WUC  = sba("WUC", [128, 2 * DC * 128], BF16)
    WDC  = sba("WDC", [128, 2 * 512], BF16)
    WRC  = sba("WRC", [128, DC * E], BF16)
    LOGF = sba("LOGF", [E, T], F32)
    LOGT = sba("LOGT", [128, NT * E], F32)
    LWORK = sba("LWORK", [128, E], F32)
    SELW = sba("SELW", [128, E], F32)
    MSCR = sba("MSCR", [128, E], F32)
    WTOK = sba("WTOK", [128, NT * E], F32)
    WTR  = sba("WTR", [E, T], BF16)
    eall = sba("eall", [32, E * 128], BF16)
    SGT  = sba("SGT", [128, TB], BF16)
    FFN_END = cur[0] - AB
    assert FFN_END <= ARENA_BYTES, FFN_END

    XF2A = sba("XF2A", [128, DC * T], BF16, at=0)
    XB0  = sba("XB0", [128, DC * T], BF16, at=32 * 1024)
    XB1  = sba("XB1", [128, DC * T], BF16, at=64 * 1024)
    QF   = sba("QF", [128, DC * T], BF16, at=96 * 1024)
    WVC  = sba("WVC", [128, DC * 1024], BF16, at=128 * 1024)
    WQKC = sba("WQKC", [128, 2 * DC * 128], BF16, at=144 * 1024)
    KVO  = sba("KVO", [128, 2 * 1024], BF16, at=148 * 1024)
    CTXF = sba("CTXF", [128, DC * T], BF16, at=0)
    KSB  = sba("KSB", [128, 2 * L], BF16, at=32 * 1024)
    VSB  = sba("VSB", [128, NG * 128], BF16, at=48 * 1024)
    ZACC = sba("ZACC", [128, 4 * ZSLOT], F32, at=56 * 1024)
    ZQT  = sba("ZQT", [128, 2], F32, at=56 * 1024 + 256)
    ZINVR = sba("ZINVR", [2, 128], F32, at=56 * 1024 + 288)
    BIASL = sba("BIASL", [128, 4], F32, at=56 * 1024 + 1024)
    BIASR = sba("BIASR", [128, 4], F32, at=56 * 1024 + 1056)
    ZBS   = sba("ZBS", [128, 128], F32, at=56 * 1024 + 1536)
    WOC  = sba("WOC", [128, 2 * 1024], BF16, at=58 * 1024)
    PSB  = sba("PSB", [128, 2 * L], BF16, at=64 * 1024)
    PTB  = sba("PTB", [128, 2 * L], BF16, at=80 * 1024)
    QREV = sba("QREV", [128, 2 * GW], BF16, at=128 * 1024)
    GPB  = sba("GPB", [128, 4 * 2304], BF16, at=138 * 1024)
    ATT_END = 156 * 1024
    assert ATT_END <= ARENA_BYTES

    pb = [nc.alloc_psum_tensor(f"pb{i}", [128, 512], F32) for i in range(8)]

    semnames = ["ld", "lwg0", "lwg1", "lwd0", "lwd1", "lwr", "st",
                "tc", "vc", "sc", "gc", "agS",
                "pt0", "pt1", "pt2", "pt3",
                "gw0", "gw1", "sk0", "sk1",
                "kv0", "kv1", "kvv", "ks0", "ks1", "ccS"]
    sems = {n: Sem(nc, n) for n in semnames}

    def program(which):
        t = Eng(nc.tensor if which == "t" else None)
        v = Eng(nc.vector if which == "v" else None, auto_drain=True)
        s = Eng(nc.scalar if which == "s" else None,
                auto_drain=os.environ.get("KRACE", "0") == "1")
        g = Eng(nc.gpsimd if which == "g" else None, auto_drain=True)
        y = Eng(nc.sync if which == "y" else None)
        for sm in sems.values():
            sm.n = 0
        ld, st = sems["ld"], sems["st"]
        lwg = [sems["lwg0"], sems["lwg1"]]
        lwd = [sems["lwd0"], sems["lwd1"]]
        lwr = sems["lwr"]
        agS = sems["agS"]
        ccS = sems["ccS"]
        pt = [sems["pt0"], sems["pt1"], sems["pt2"], sems["pt3"]]
        gw = [sems["gw0"], sems["gw1"]]
        sk = [sems["sk0"], sems["sk1"]]
        kvs = [sems["kv0"], sems["kv1"]]
        kvv = sems["kvv"]
        ks = [sems["ks0"], sems["ks1"]]
        tc, vc, sc, gc = sems["tc"], sems["vc"], sems["sc"], sems["gc"]

        def gdma(eng, sem, **kw):
            eng.wait(sem, sem.n)
            sem.inc(eng.dma_start(**kw), 16)

        for ins_args in [(onesr, 1.0), (onestf, 1.0), (onestb, 1.0),
                         (epsln, EPS_LN), (negCt, -C_EXP), (identb, 0.0),
                         (identf, 0.0), (sel2, 0.0), (eall, 0.0)]:
            gc.inc(g.memset(ins_args[0][:, :], ins_args[1]), 1)
        g.wait(gc, gc.n)
        gc.inc(g.affine_select(identb[:, :], onestb[:, 0:128],
                               [[-1, 128]], ALU.is_equal, 0.0,
                               base=0, channel_multiplier=1), 1)
        gc.inc(g.affine_select(identf[:, :], onestf[:, 0:128],
                               [[-1, 128]], ALU.is_equal, 0.0,
                               base=0, channel_multiplier=1), 1)
        gc.inc(g.affine_select(sel2[0:2, 0:64], onestf[0:2, 0:64], [[0, 64]],
                               ALU.is_equal, 0.0, base=0, channel_multiplier=1), 1)
        gc.inc(g.affine_select(sel2[0:2, 64:128], onestf[0:2, 0:64], [[0, 64]],
                               ALU.is_equal, 0.0, base=-1, channel_multiplier=1), 1)
        GC_INIT = gc.n

        for tt in range(NT):
            ld.inc(g.dma_start(out=X[:, tt * D:(tt + 1) * D], in_=x_in[tt]), 16)
        ld.inc(g.dma_start(out=relsb[:, :], in_=relr_d.ap()), 16)
        ld.inc(g.dma_start(out=rbias[:, :], in_=rb_d.ap()), 16)
        for bia, dsrc in [(biasq, bq_d), (biask, bk_d), (biasv, bv_d), (biaso, bo_d)]:
            ld.inc(g.dma_start(out=bia[:, :], in_=dsrc.ap()), 16)
        X_READY = ld.n

        def ln_all(wait_resid_vc, dst_f):
            s.wait(vc, wait_resid_vc)
            v.wait(vc, 0)
            for tt in range(NT):
                xs = X[:, tt * D:(tt + 1) * D]
                dst = LNTOK[:, tt * D:(tt + 1) * D]
                sc.inc(s.activation(dst, xs, AF.Square, accum_out=TMPS[:, 1:2]), 1)
                v.tensor_reduce(TMPS[:, 0:1], xs, AX.X, ALU.add, negate=True)
                v.wait(sc, sc.n)
                v.tensor_scalar(TMPS[:, 2:3], TMPS[:, 0:1], 1.0 / D, None, ALU.mult)
                v.tensor_tensor(TMPS[:, 3:4], TMPS[:, 2:3], TMPS[:, 2:3], ALU.mult)
                vc.inc(v.scalar_tensor_tensor(TMPS[:, 4:5], TMPS[:, 1:2], 1.0 / D,
                                              TMPS[:, 3:4], ALU.mult, ALU.subtract), 1)
                s.wait(vc, vc.n)
                sc.inc(s.activation(TMPS[:, 5:6], TMPS[:, 4:5], AF.Sqrt,
                                    bias=epsln[:, :]), 1)
                v.wait(sc, sc.n)
                v.reciprocal(TMPS[:, 6:7], TMPS[:, 5:6])
                vc.inc(v.tensor_scalar(dst, xs, TMPS[:, 2:3], TMPS[:, 6:7],
                                       ALU.add, ALU.mult), 1)
            LN_DONE = vc.n
            bank_free_at = [0, 0]
            for i in range(NT * DC):
                tt, c = divmod(i, DC)
                bk = pb[6 + (i % 2)]
                tdst = bk.ap().bitcast(BF16)[:, 0:128]
                if i == 0:
                    t.wait(vc, LN_DONE)
                t.wait(vc, bank_free_at[i % 2])
                tc.inc(t.transpose(tdst, LNTOK[:, tt * D + c * 128: tt * D + (c + 1) * 128],
                                   identb[:, :]), 1)
                v.wait(tc, tc.n)
                vc.inc(v.tensor_copy(dst_f[:, c * T + tt * 128: c * T + (tt + 1) * 128],
                                     tdst), 1)
                bank_free_at[i % 2] = vc.n
            return vc.n

        def ffn(call, wait_resid_vc):
            XLF_READY = ln_all(wait_resid_vc, XLF)

            gdma(g, lwr, out=WRC[:, :], in_=wr_d.ap())
            WR_LD = lwr.n
            t.wait(lwr, WR_LD)
            t.wait(vc, XLF_READY)
            for tkb in range(4):
                for dc in range(DC):
                    ins = t.matmul(pb[6][0:E, :], WRC[:, dc * E:(dc + 1) * E],
                                   XLF[:, dc * T + tkb * 512: dc * T + (tkb + 1) * 512],
                                   start=(dc == 0), stop=(dc == DC - 1))
                tc.inc(ins, 1)
                v.wait(tc, tc.n)
                vc.inc(v.tensor_scalar(LOGF[:, tkb * 512:(tkb + 1) * 512],
                                       pb[6][0:E, :], rbias[:, :], None, ALU.add), 1)
                t.wait(vc, vc.n)
            bank_free_at = [vc.n, vc.n]
            for tt in range(NT):
                bk = pb[6 + (tt % 2)]
                t.wait(vc, bank_free_at[tt % 2])
                tc.inc(t.transpose(bk.ap()[:, 0:E],
                                   LOGF[:, tt * 128:(tt + 1) * 128],
                                   identf[0:E, 0:E]), 1)
                v.wait(tc, tc.n)
                vc.inc(v.tensor_copy(LOGT[:, tt * E:(tt + 1) * E], bk.ap()[0:128, 0:E]), 1)
                bank_free_at[tt % 2] = vc.n

            for tt in range(NT):
                l0 = LOGT[:, tt * E:(tt + 1) * E]
                v.tensor_copy(LWORK[:, :], l0)
                v.memset(SELW[:, :], 0.0)
                for r in range(4):
                    v.tensor_reduce(TMPS[:, 8:9], LWORK[:, :], AX.X, ALU.max,
                                    negate=True)
                    v.tensor_scalar(MSCR[:, :], LWORK[:, :], TMPS[:, 8:9], 0.0,
                                    ALU.add, ALU.is_ge)
                    v.tensor_add(SELW[:, :], SELW[:, :], MSCR[:, :])
                    if r == 0:
                        v.tensor_copy(TMPS[:, 9:10], TMPS[:, 8:9])
                    if r < 3:
                        v.scalar_tensor_tensor(LWORK[:, :], MSCR[:, :], -30000.0,
                                               LWORK[:, :], ALU.mult, ALU.add)
                vc.inc(v.engine_nop(), 1)
                s.wait(vc, vc.n)
                sc.inc(s.activation(LWORK[:, :], l0, AF.Exp, bias=TMPS[:, 9:10]), 1)
                v.wait(sc, sc.n)
                v.tensor_mul(LWORK[:, :], LWORK[:, :], SELW[:, :])
                v.tensor_reduce(TMPS[:, 10:11], LWORK[:, :], AX.X, ALU.add)
                v.reciprocal(TMPS[:, 11:12], TMPS[:, 10:11])
                vc.inc(v.tensor_scalar(WTOK[:, tt * E:(tt + 1) * E], LWORK[:, :],
                                       TMPS[:, 11:12], None, ALU.mult), 1)
            bank_free_at = [vc.n, vc.n]
            for tt in range(NT):
                bk = pb[6 + (tt % 2)]
                t.wait(vc, bank_free_at[tt % 2])
                tc.inc(t.transpose(bk.ap()[0:E, 0:128],
                                   WTOK[:, tt * E:(tt + 1) * E], identf[:, :]), 1)
                v.wait(tc, tc.n)
                vc.inc(v.tensor_copy(WTR[:, tt * 128:(tt + 1) * 128],
                                     bk.ap()[0:E, 0:128]), 1)
                bank_free_at[tt % 2] = vc.n
            WTR_READY = vc.n

            g.wait(vc, wait_resid_vc)
            gc.inc(g.memset(eall[:, :], 0.0), 1)
            g.wait(gc, gc.n)
            for e in range(E):
                gc.inc(g.affine_select(eall[:, e * 128:(e + 1) * 128],
                                       onestb[0:32, 0:128], [[0, 128]],
                                       ALU.is_equal, 0.0,
                                       base=-e, channel_multiplier=1), 1)
            EALL_RDY = gc.n

            step_consumed = {}
            down_evac_end = vc.n
            for tb in range(NTB):
                tbo = tb * TB
                for hc in range(HC):
                    wpar = hc % 2
                    it = tb * HC + hc
                    if it >= 2:
                        g.wait(vc, step_consumed[2 * (it - 2) + 1])
                    g.wait(lwg[wpar], lwg[wpar].n)
                    lwg[wpar].inc(g.dma_start(out=WGC[:, wpar * 1024:(wpar + 1) * 1024],
                                       in_=wg_d[hc]), 16)
                    lwg[wpar].inc(g.dma_start(out=WUC[:, wpar * 1024:(wpar + 1) * 1024],
                                       in_=wu_d[hc]), 16)
                    W_LD = lwg[wpar].n
                    t.wait(lwg[wpar], W_LD)
                    if it == 0:
                        t.wait(gc, EALL_RDY)
                        t.wait(vc, WTR_READY)
                        t.wait(vc, down_evac_end)
                    for th in range(2):
                        stp = 2 * it + th
                        sl = stp % 2
                        gbank, ubank, wbank = pb[0 + 2 * sl], pb[1 + 2 * sl], pb[4 + sl]
                        if stp >= 2:
                            t.wait(vc, step_consumed[stp - 2])
                        mvx = XLF
                        co = tbo + th * 512
                        if hc < E:
                            t.matmul(wbank.ap(), eall[:, hc * 128:(hc + 1) * 128],
                                     WTR[0:E, co:co + 512], start=True, stop=True)
                        for dc in range(DC):
                            t.matmul(gbank.ap(),
                                     WGC[:, wpar * 1024 + dc * 128: wpar * 1024 + (dc + 1) * 128],
                                     mvx[:, dc * T + co: dc * T + co + 512],
                                     start=(dc == 0), stop=(dc == DC - 1))
                        glast = None
                        for dc in range(DC):
                            glast = t.matmul(ubank.ap(),
                                     WUC[:, wpar * 1024 + dc * 128: wpar * 1024 + (dc + 1) * 128],
                                     mvx[:, dc * T + co: dc * T + co + 512],
                                     start=(dc == 0), stop=(dc == DC - 1))
                        tc.inc(glast, 1)
                        GU_DONE = tc.n
                        s.wait(tc, GU_DONE)
                        sc.inc(s.activation(SGT[:, th * 512:(th + 1) * 512],
                                            gbank.ap(), AF.Silu), 1)
                        v.wait(sc, sc.n)
                        hss = HFB[:, hc * TB + th * 512: hc * TB + (th + 1) * 512]
                        v.scalar_tensor_tensor(hss, ubank.ap(), 1.0,
                                               SGT[:, th * 512:(th + 1) * 512],
                                               ALU.mult, ALU.mult)
                        if hc < E:
                            vins = v.tensor_tensor(hss, hss, wbank.ap(), ALU.mult)
                        else:
                            vins = v.engine_nop()
                        vc.inc(vins, 1)
                        step_consumed[stp] = vc.n
                H_DONE = vc.n

                wd_consumed = {}
                for half in range(2):
                    for hc in range(HC):
                        wpar = hc % 2
                        ih = half * HC + hc
                        if ih >= 2:
                            g.wait(tc, wd_consumed[ih - 2])
                        gdma(g, lwd[wpar], out=WDC[:, wpar * 512:(wpar + 1) * 512],
                             in_=wd_d[half, hc])
                        WD_LD = lwd[wpar].n
                        t.wait(lwd[wpar], WD_LD)
                        if hc == 0:
                            t.wait(vc, H_DONE if half == 0 else vc.n)
                        ins = None
                        for tt8 in range(8):
                            ins = t.matmul(pb[tt8].ap(),
                                           HFB[:, hc * TB + tt8 * 128:
                                               hc * TB + (tt8 + 1) * 128],
                                           WDC[:, wpar * 512:(wpar + 1) * 512],
                                           start=(hc == 0), stop=(hc == HC - 1))
                        tc.inc(ins, 1)
                        wd_consumed[ih] = tc.n
                    DOWN_DONE = tc.n
                    v.wait(tc, DOWN_DONE)
                    for tt8 in range(8):
                        gt = tb * 8 + tt8
                        xsl = X[:, gt * D + half * 512: gt * D + (half + 1) * 512]
                        vc.inc(v.scalar_tensor_tensor(xsl, pb[tt8].ap(), 0.5,
                                                      xsl, ALU.mult, ALU.add), 1)
                    down_evac_end = vc.n
                    t.wait(vc, down_evac_end)
            return vc.n

        def attention(wait_resid_vc):
            XF2_READY = ln_all(wait_resid_vc, XF2A)

            g.wait(vc, XF2_READY)
            agS.inc(g.dma_start(out=ag_in.ap(), in_=XF2A[:, :]), 16)
            g.wait(agS, agS.n)
            ccS.inc(g.collective_compute(
                "AllGather", ALU.bypass,
                replica_groups=[[0, 1], [2, 3], [4, 5], [6, 7]],
                ins=[ag_in.ap().opt()], outs=[ag_out.ap().opt()]), 1)
            g.wait(ccS, ccS.n)
            agS.inc(g.dma_start(out=XB0[:, :], in_=ag_out[0]), 16)
            agS.inc(g.dma_start(out=XB1[:, :], in_=ag_out[1]), 16)
            XF2B_READY = agS.n

            t.wait(agS, XF2B_READY)
            t.wait(vc, XF2_READY)
            gdma(g, lwr,
                 out=bass.AP(WVC, 0, [[DC * 1024, 128], [1024, DC], [1, 1024]]),
                 in_=bass.AP(wv_d, 0, [[1024, 128], [131072, DC], [1, 1024]]))
            WVC_LD = lwr.n
            qk_cons = {}
            for qc in range(DC):
                wpar = qc % 2
                if qc >= 2:
                    g.wait(tc, qk_cons[qc - 2])
                gdma(g, lwg[wpar],
                     out=WQKC[:, wpar * 1024:(wpar + 1) * 1024], in_=wq_d[qc])
                t.wait(lwg[wpar], lwg[wpar].n)
                for tkb in range(4):
                    bk = pb[tkb % 4]
                    for dc in range(DC):
                        t.matmul(bk.ap(),
                                 WQKC[:, wpar * 1024 + dc * 128: wpar * 1024 + (dc + 1) * 128],
                                 XF2A[:, dc * T + tkb * 512: dc * T + (tkb + 1) * 512],
                                 start=(dc == 0), stop=False)
                    tc.inc(t.matmul(bk.ap(), biasq[0:1, qc * 128:(qc + 1) * 128],
                                    onesr[0:1, 0:512], start=False, stop=True), 1)
                    v.wait(tc, tc.n)
                    vc.inc(v.tensor_copy(
                        QF[:, qc * T + tkb * 512: qc * T + (tkb + 1) * 512], bk.ap()), 1)
                    t.wait(vc, vc.n)
                qk_cons[qc] = tc.n
            QF_READY = vc.n
            kv_ev = 0
            for qc in range(DC):
                wpar = qc % 2
                g.wait(tc, qk_cons[DC + qc - 2] if qc >= 2 else qk_cons[DC - 1])
                gdma(g, lwg[wpar],
                     out=WQKC[:, wpar * 1024:(wpar + 1) * 1024], in_=wk_d[qc])
                t.wait(lwg[wpar], lwg[wpar].n)
                for half in range(2):
                    xsrc = XB0 if half == 0 else XB1
                    for grp in range(4):
                        bk = pb[(half * 4 + grp) % 4]
                        for dc in range(DC):
                            t.matmul(bk.ap(),
                                     WQKC[:, wpar * 1024 + dc * 128: wpar * 1024 + (dc + 1) * 128],
                                     xsrc[:, dc * T + grp * 512: dc * T + (grp + 1) * 512],
                                     start=(dc == 0), stop=False)
                        tc.inc(t.matmul(bk.ap(), biask[0:1, qc * 128:(qc + 1) * 128],
                                        onesr[0:1, 0:512], start=False, stop=True), 1)
                        kb = kv_ev % 2
                        v.wait(tc, tc.n)
                        v.wait(ks[kb], ks[kb].n)
                        vc.inc(v.tensor_scalar(KVO[:, kb * 1024: kb * 1024 + 512],
                                               bk.ap(), 0.125, None, ALU.mult), 1)
                        t.wait(vc, vc.n)
                        g.wait(vc, vc.n)
                        off = qc * 524288 + grp * 1024 + half * 128
                        gdma(g, ks[kb],
                             out=bass.AP(k_hbm, off, [[4096, 128], [256, 4], [1, 128]]),
                             in_=bass.AP(KVO, kb * 1024,
                                         [[2048, 128], [128, 4], [1, 128]]))
                        kv_ev += 1
                qk_cons[DC + qc] = tc.n
            t.wait(lwr, WVC_LD)
            for half in range(2):
                xsrc = XB0 if half == 0 else XB1
                for lt in range(NT):
                    bks = [pb[(2 * lt) % 4], pb[(2 * lt) % 4 + 1]]
                    for vh in range(2):
                        bk = bks[vh]
                        for dc in range(DC):
                            t.matmul(bk.ap(),
                                     xsrc[:, dc * T + lt * 128: dc * T + (lt + 1) * 128],
                                     WVC[:, dc * 1024 + vh * 512: dc * 1024 + (vh + 1) * 512],
                                     start=(dc == 0), stop=False)
                        tc.inc(t.matmul(bk.ap(), onesr[0:1, 0:128],
                                        biasv[0:1, vh * 512:(vh + 1) * 512],
                                        start=False, stop=True), 1)
                    kb = kv_ev % 2
                    v.wait(tc, tc.n)
                    v.wait(ks[kb], ks[kb].n)
                    v.tensor_copy(KVO[:, kb * 1024: kb * 1024 + 512], bks[0].ap())
                    vc.inc(v.tensor_copy(KVO[:, kb * 1024 + 512: kb * 1024 + 1024],
                                         bks[1].ap()), 1)
                    t.wait(vc, vc.n)
                    g.wait(vc, vc.n)
                    gdma(g, ks[kb],
                         out=v_hbm[2 * lt + half],
                         in_=KVO[:, kb * 1024: kb * 1024 + 1024])
                    kv_ev += 1
            KS_FIN = [ks[0].n, ks[1].n]
            QKV_T_DONE = tc.n

            g.wait(ks[0], KS_FIN[0])
            g.wait(ks[1], KS_FIN[1])

            t.wait(vc, QF_READY)
            C_QREL = [(c * 512, min(NRELX, (c + 1) * 512)) for c in range(5)]
            st_qrel_vc = 0
            st_gw = 0
            st_relmm = {0: 0, 1: 0}
            st_ctx_free = {0: 0, 1: 0}
            st_pb5_free = 0
            st_last_pv = 0
            st_trp_free = {0: 0, 1: 0}
            st_psb_free = {0: 0, 1: 0}
            st_ptb_free = 0
            ksb_free = {}
            vsb_free = {}

            for hp in range(8):
                kb = hp % 2
                if hp >= 2:
                    g.wait(tc, ksb_free[hp - 2])
                gdma(g, kvs[kb], out=KSB[:, kb * L:(kb + 1) * L], in_=k_hbm[hp])
                KSB_LD = kvs[kb].n
                if hp >= 1:
                    g.wait(tc, vsb_free[hp - 1])
                gdma(g, kvv,
                     out=bass.AP(VSB, 0, [[NG * 128, 128], [128, NG], [1, 128]]),
                     in_=bass.AP(v_hbm, hp * 128,
                                 [[1024, 128], [131072, NG], [1, 128]]))
                VSB_LD = kvv.n

                qrel_ev = {}
                sk_mark = {}
                exp_mark = {}
                pv_last = {}
                ctx_ev = {}
                ptb_ready = {}

                def emit_qrel(lt):
                    nonlocal st_qrel_vc, st_gw
                    qbuf = lt % 2
                    jlo, jhi = win(lt)
                    v.wait(gw[0], st_gw)
                    for hip in range(2):
                        for c0, c1 in C_QREL:
                            t.wait(vc, st_qrel_vc)
                            tc.inc(t.matmul(
                                pb[4][:, 0:c1 - c0],
                                QF[hip * 64:(hip + 1) * 64,
                                   hp * T + lt * 128: hp * T + (lt + 1) * 128],
                                relsb[hip * 64:(hip + 1) * 64, c0:c1],
                                start=True, stop=True,
                                tile_position=(hip * 64, 0)), 1)
                            v.wait(tc, tc.n)
                            st_qrel_vc = vc.inc(v.tensor_copy(
                                QREV[:, hip * GW + PAD + c0: hip * GW + PAD + c1],
                                pb[4][:, 0:c1 - c0]), 1)
                        v.tensor_copy(TMPS[:, 12:13],
                                      QREV[:, hip * GW + PAD: hip * GW + PAD + 1])
                        v.tensor_copy(TMPS[:, 13:14],
                                      QREV[:, hip * GW + PAD + NRELX - 1: hip * GW + PAD + NRELX])
                        v.tensor_scalar(QREV[:, hip * GW: hip * GW + PAD],
                                        onestb[:, 0:PAD],
                                        TMPS[:, 12:13], None, ALU.mult)
                        v.tensor_scalar(
                            QREV[:, hip * GW + PAD + NRELX: hip * GW + GW],
                            onestb[:, 0:GW - PAD - NRELX],
                            TMPS[:, 13:14], None, ALU.mult)
                        v.tensor_scalar(BIASL[:, qbuf * 2 + hip: qbuf * 2 + hip + 1],
                                        QREV[:, hip * GW + PAD: hip * GW + PAD + 1],
                                        -C_EXP, None, ALU.add)
                        v.tensor_scalar(BIASR[:, qbuf * 2 + hip: qbuf * 2 + hip + 1],
                                        QREV[:, hip * GW + PAD + NRELX - 1: hip * GW + PAD + NRELX],
                                        -C_EXP, None, ALU.add)
                    qrel_ev[lt] = vc.inc(v.engine_nop(), 1)
                    g.wait(vc, qrel_ev[lt])
                    g.wait(sk[qbuf], sk[qbuf].n)
                    st_gw = gw[0].inc(g.dma_start(out=g_hbm[qbuf], in_=QREV[:, :]), 16)

                def emit_skew(lt):
                    qbuf = lt % 2
                    jlo, jhi = win(lt)
                    g.wait(gw[0], st_gw)
                    g.wait(tc, st_relmm[qbuf])
                    i0c = 256 * lt
                    W = jhi - jlo
                    g.wait(sk[qbuf], sk[qbuf].n)
                    for hip in range(2):
                        base = qbuf * 128 * (2 * GW) + hip * GW + (1278 - i0c + jlo)
                        sk[qbuf].inc(g.dma_start(
                            out=GPB[:, (qbuf * 2 + hip) * 2304: (qbuf * 2 + hip) * 2304 + W],
                            in_=bass.AP(g_hbm, base,
                                        [[2 * GW - 1, 128], [1, W]])), 16)
                    sk_mark[lt] = sk[qbuf].n

                def emit_pv_chunk(lq, kc):
                    cbk = pb[6 + lq % 2]
                    if kc == 0:
                        t.wait(vc, ptb_ready[lq])
                        t.wait(kvv, VSB_LD)
                        t.wait(vc, st_ctx_free[lq % 2])
                    t.matmul(cbk[0:64, 0:128],
                             VSB[:, kc * 128: kc * 128 + 64],
                             PTB[:, 0 * L + kc * 128: 0 * L + (kc + 1) * 128],
                             start=(kc == 0), stop=(kc == NG - 1),
                             tile_position=(0, 0), skip_group_check=True)
                    lastm = t.matmul(cbk[64:128, 0:128],
                             VSB[:, kc * 128 + 64: (kc + 1) * 128],
                             PTB[:, 1 * L + kc * 128: 1 * L + (kc + 1) * 128],
                             start=(kc == 0), stop=(kc == NG - 1),
                             tile_position=(0, 64), skip_group_check=True)
                    if kc == NG - 1:
                        pv_last[lq] = tc.inc(lastm, 1)
                        st_last_pv_set(pv_last[lq])

                def st_last_pv_set(vv):
                    nonlocal st_last_pv
                    st_last_pv = vv

                def emit_z(lq):
                    nonlocal st_pb5_free
                    qb = lq % 2
                    cbk = pb[6 + qb]
                    v.wait(sc, exp_all_m[(1, lq)])
                    v.tensor_reduce(ZQT[:, 0:1],
                                    ZACC[:, qb * 2 * ZSLOT: qb * 2 * ZSLOT + ZSLOT],
                                    AX.X, ALU.add)
                    v.tensor_reduce(ZQT[:, 1:2],
                                    ZACC[:, qb * 2 * ZSLOT + ZSLOT: (qb + 1) * 2 * ZSLOT],
                                    AX.X, ALU.add)
                    zr = vc.inc(v.tensor_scalar(ZQT[:, :], ZQT[:, :],
                                                1.0 + 1e-8, None, ALU.mult), 1)
                    t.wait(vc, zr)
                    t.wait(vc, st_pb5_free)
                    zt = tc.inc(t.transpose(pb[5][0:2, 0:128], ZQT[:, :],
                                            identf[:, :]), 1)
                    v.wait(tc, zt)
                    rcp = vc.inc(v.reciprocal(ZINVR[0:2, :], pb[5][0:2, 0:128]), 1)
                    t.wait(vc, rcp)
                    zb = tc.inc(t.matmul(pb[5][:, 128:256], sel2[0:2, :],
                                         ZINVR[0:2, :], start=True, stop=True), 1)
                    v.wait(tc, zb)
                    v.wait(tc, pv_last[lq])
                    v.tensor_copy(ZBS[:, :], pb[5][:, 128:256])
                    ctx_ev[lq] = vc.inc(v.tensor_tensor(
                        CTXF[:, hp * T + lq * 128: hp * T + (lq + 1) * 128],
                        cbk[:, 0:128], ZBS[:, :], ALU.mult), 1)
                    st_ctx_free[qb] = ctx_ev[lq]
                    st_pb5_free = ctx_ev[lq]

                emit_qrel(0)
                emit_skew(0)
                emit_qrel(1)
                emit_skew(1)
                exp_all_m = {}

                for qt in range(NT):
                    qbuf = qt % 2
                    jlo, jhi = win(qt)
                    if qt + 2 < NT:
                        emit_qrel(qt + 2)
                    sc0 = sc.n
                    zm = vc.inc(v.memset(ZACC[:, qbuf * 2 * ZSLOT:
                                              (qbuf + 1) * 2 * ZSLOT], 0.0), 1)
                    for kc in range(8):
                        c0, c1 = kc * 512, (kc + 1) * 512
                        b0, b1 = max(c0, jlo), min(c1, jhi)
                        smm = {}
                        t.wait(sc, sc0 + 2 * kc)
                        if kc == 0:
                            t.wait(kvs[kb], KSB_LD)
                            t.wait(sk[qbuf], sk_mark[qt])
                        for hip in range(2):
                            smm[hip] = t.matmul(
                                pb[hip].ap(),
                                QF[hip * 64:(hip + 1) * 64,
                                   hp * T + qt * 128: hp * T + (qt + 1) * 128],
                                KSB[hip * 64:(hip + 1) * 64, kb * L + c0: kb * L + c1],
                                start=True, stop=(b0 >= b1),
                                tile_position=(hip * 64, 0))
                        for hip in range(2):
                            lastm = smm[hip]
                            if b0 < b1:
                                lastm = t.matmul(
                                    pb[hip][:, b0 - c0: b1 - c0],
                                    identb[:, :],
                                    GPB[:, (qbuf * 2 + hip) * 2304 + (b0 - jlo):
                                        (qbuf * 2 + hip) * 2304 + (b1 - jlo)],
                                    start=False, stop=True)
                            exp_mark[(hip, qt, kc)] = tc.inc(lastm, 1)
                        if qt >= 1:
                            for j in range(4):
                                emit_pv_chunk(qt - 1, 4 * kc + j)
                    st_relmm[qbuf] = tc.n

                    s.wait(vc, zm)
                    s.wait(vc, qrel_ev[qt])
                    s.wait(tc, st_psb_free[0])
                    s.wait(tc, st_psb_free[1])
                    slot = [0, 0]
                    for kc in range(8):
                        c0, c1 = kc * 512, (kc + 1) * 512
                        for hip in range(2):
                            bkS = pb[hip]
                            s.wait(tc, exp_mark[(hip, qt, kc)])
                            segs = []
                            if c0 < jlo:
                                segs.append((c0, min(c1, jlo),
                                             BIASL[:, qbuf * 2 + hip: qbuf * 2 + hip + 1]))
                            if max(c0, jlo) < min(c1, jhi):
                                segs.append((max(c0, jlo), min(c1, jhi), negCt[:, :]))
                            if jhi < c1:
                                segs.append((max(c0, jhi), c1,
                                             BIASR[:, qbuf * 2 + hip: qbuf * 2 + hip + 1]))
                            last = None
                            for (s0, s1, bias_ap) in segs:
                                last = s.activation(
                                    PSB[:, hip * L + s0: hip * L + s1],
                                    bkS[:, s0 - c0: s1 - c0],
                                    AF.Exp, bias=bias_ap,
                                    accum_out=ZACC[:, qbuf * 2 * ZSLOT + hip * ZSLOT + slot[hip]:
                                                   qbuf * 2 * ZSLOT + hip * ZSLOT + slot[hip] + 1])
                                slot[hip] += 1
                            sc.inc(last, 1)
                    exp_all_m[(0, qt)] = sc.n
                    exp_all_m[(1, qt)] = sc.n

                    if qt >= 1:
                        emit_z(qt - 1)

                    for hip in range(2):
                        for j4 in range(NG // 4):
                            bkT = pb[2 + j4 % 2]
                            t.wait(vc, st_trp_free[j4 % 2])
                            t.wait(sc, sc0 + 2 * j4 + hip + 1)
                            if hip == 0 and j4 == 0:
                                t.wait(tc, pv_last[qt - 1] if qt >= 1 else st_ptb_free)
                            lastm = None
                            for jj in range(4):
                                j = j4 * 4 + jj
                                lastm = t.transpose(
                                    bkT.ap().bitcast(BF16)[:, jj * 128:(jj + 1) * 128],
                                    PSB[:, hip * L + j * 128: hip * L + (j + 1) * 128],
                                    identb[:, :])
                            tc.inc(lastm, 1)
                            if j4 == NG // 4 - 1:
                                st_psb_free[hip] = tc.n
                            v.wait(tc, tc.n)
                            st_trp_free[j4 % 2] = vc.inc(v.tensor_copy(
                                PTB[:, hip * L + j4 * 512: hip * L + (j4 + 1) * 512],
                                bkT.ap().bitcast(BF16)[:, 0:512]), 1)
                    ptb_ready[qt] = vc.n
                    if qt + 2 < NT:
                        emit_skew(qt + 2)

                for kc in range(NG):
                    emit_pv_chunk(NT - 1, kc)
                emit_z(NT - 1)
                st_ptb_free = pv_last[NT - 1]
                ksb_free[hp] = tc.n
                vsb_free[hp] = tc.n

            op_bank_free = {}
            for grp in range(2):
                for half in range(2):
                    for dc in range(DC):
                        wpar = dc % 2
                        gi = (grp * 2 + half) * DC + dc
                        g.wait(tc, op_bank_free.get(gi - 2, 0))
                        gdma(g, lwg[wpar],
                             out=WOC[:, wpar * 1024: wpar * 1024 + 512],
                             in_=wo_d.ap()[dc * 128:(dc + 1) * 128,
                                           half * 512:(half + 1) * 512])
                        t.wait(lwg[wpar], lwg[wpar].n)
                        if dc == 0:
                            t.wait(vc, vc.n)
                        for q8 in range(8):
                            qt = grp * 8 + q8
                            lastm = t.matmul(
                                pb[q8].ap(),
                                CTXF[:, dc * T + qt * 128: dc * T + (qt + 1) * 128],
                                WOC[:, wpar * 1024: wpar * 1024 + 512],
                                start=(dc == 0), stop=False)
                        if dc == DC - 1:
                            for q8 in range(8):
                                lastm = t.matmul(pb[q8].ap(), onesr[0:1, 0:128],
                                                 biaso[0:1, half * 512:(half + 1) * 512],
                                                 start=False, stop=True)
                        tc.inc(lastm, 1)
                        op_bank_free[gi] = tc.n
                    v.wait(tc, tc.n)
                    for q8 in range(8):
                        qt = grp * 8 + q8
                        xsl = X[:, qt * D + half * 512: qt * D + (half + 1) * 512]
                        vc.inc(v.tensor_tensor(xsl, xsl, pb[q8].ap(), ALU.add), 1)
                    t.wait(vc, vc.n)
            return vc.n

        t.wait(gc, GC_INIT); v.wait(gc, GC_INIT); s.wait(gc, GC_INIT); y.wait(gc, GC_INIT)
        t.wait(ld, X_READY); v.wait(ld, X_READY); s.wait(ld, X_READY)
        resid_vc = 0
        resid_vc = ffn(0, resid_vc)
        resid_vc = attention(resid_vc)
        resid_vc = ffn(1, resid_vc)

        g.wait(vc, resid_vc)
        g.wait(tc, sems["tc"].n)
        for tt in range(NT):
            st.inc(g.dma_start(out=out_d[tt], in_=X[:, tt * D:(tt + 1) * D]), 16)
        g.wait(st, st.n)

    with nc.Block() as blk:
        blk.tensor(lambda e: program("t"))
        blk.vector(lambda e: program("v"))
        blk.scalar(lambda e: program("s"))
        blk.gpsimd(lambda e: program("g"))
        blk.sync(lambda e: program("y"))
    return nc


def _prep_weights(inputs):
    f32 = np.float32
    wq = np.ascontiguousarray(inputs["L0_attn_Wq"], dtype=f32)
    wk = np.ascontiguousarray(inputs["L0_attn_Wk"], dtype=f32)

    def qk_layout(w):
        w4 = w.reshape(DC, 128, DC, 128)
        return np.ascontiguousarray(w4.transpose(2, 1, 0, 3).reshape(DC, 128, DC * 128))

    gate = np.asarray(inputs["L0_ffn_gate_w"], dtype=f32).reshape(4096, D)
    up = np.asarray(inputs["L0_ffn_up_w"], dtype=f32).reshape(4096, D)
    sgate = np.asarray(inputs["L0_ffn_sgate_w"], dtype=f32)
    sup = np.asarray(inputs["L0_ffn_sup_w"], dtype=f32)
    down = np.asarray(inputs["L0_ffn_down_w"], dtype=f32).transpose(0, 2, 1).reshape(4096, D)
    sdown = np.asarray(inputs["L0_ffn_sdown_w"], dtype=f32).T

    gall = np.concatenate([gate, sgate], 0)
    uall = np.concatenate([up, sup], 0)
    dall = np.concatenate([down, sdown], 0)

    def gu_layout(w):
        w4 = w.reshape(HC, 128, DC, 128)
        w4 = w.reshape(HC, 128, DC, 128)
        return np.ascontiguousarray(w4.transpose(0, 3, 2, 1).reshape(HC, 128, DC * 128))

    def down_layout(w):
        w4 = w.reshape(HC, 128, 2, 512)
        return np.ascontiguousarray(w4.transpose(2, 0, 1, 3))

    router = np.asarray(inputs["L0_ffn_router_w"], dtype=f32)
    wr = np.ascontiguousarray(router.reshape(E, DC, 128).transpose(2, 1, 0)
                              .reshape(128, DC * E))

    rel = np.asarray(inputs["L0_attn_rel_emb"], dtype=f32)
    rel_rev = rel[::-1]
    relx = {}
    for par in (0, 1):
        idx = np.clip(np.arange(NRELX) - 128 * par, 0, NREL - 1)
        rx = rel_rev[idx].T
        relx[par] = np.ascontiguousarray(np.concatenate([rx, rx], axis=0))

    out = {
        "wq": qk_layout(wq), "wk": qk_layout(wk),
        "wv": np.ascontiguousarray(inputs["L0_attn_Wv"], dtype=f32),
        "wo": np.ascontiguousarray(inputs["L0_attn_Wo"], dtype=f32),
        "bq": np.ascontiguousarray(inputs["L0_attn_bq"], dtype=f32).reshape(1, D),
        "bk": np.ascontiguousarray(inputs["L0_attn_bk"], dtype=f32).reshape(1, D),
        "bv": np.ascontiguousarray(inputs["L0_attn_bv"], dtype=f32).reshape(1, D),
        "bo": np.ascontiguousarray(inputs["L0_attn_bo"], dtype=f32).reshape(1, D),
        "wr": wr,
        "rb": np.ascontiguousarray(inputs["L0_ffn_router_bias"], dtype=f32).reshape(E, 1),
        "wg": gu_layout(gall), "wu": gu_layout(uall), "wd": down_layout(dall),
    }
    return out, relx


_NC_CACHE = {}

def kernel(**inputs):
    if "prog" not in _NC_CACHE:
        _NC_CACHE["prog"] = build_program()
    nc = _NC_CACHE["prog"]
    w, relx = _prep_weights(inputs)
    x = np.asarray(inputs["L0_attn_x"], dtype=np.float32)
    in_maps = []
    for c in range(8):
        b, par = c // 2, c % 2
        xt = np.ascontiguousarray(
            x[b].reshape(NG, 128, D)[par::2])
        im = dict(w)
        im["x"] = xt
        im["relr"] = relx[par]
        in_maps.append(im)
    res = run_bass_kernel_spmd(nc, in_maps, core_ids=list(range(8)))
    out = np.zeros((B, NG, 128, D), dtype=np.float32)
    for c in range(8):
        b, par = c // 2, c % 2
        out[b, par::2] = res.results[c]["out"]
    return out.reshape(B, L, D)


# version 12 (final)
# speedup vs baseline: 1.0687x; 1.0432x over previous
import os
import numpy as np
import ml_dtypes
import concourse.bass as bass
import concourse.mybir as mybir
from concourse.bass_utils import run_bass_kernel_spmd
import concourse.bass_utils as _bu

_orig_run_command = _bu.run_command
def _run_command_ldw(argv, **kw):
    argv = ["--enable-ldw-opt=true" if a == "--enable-ldw-opt=false" else a
            for a in argv]
    return _orig_run_command(argv, **kw)
_bu.run_command = _run_command_ldw

F32 = mybir.dt.float32
BF16 = mybir.dt.bfloat16
U32 = mybir.dt.uint32
AF = mybir.ActivationFunctionType
ALU = mybir.AluOpType
AX = mybir.AxisListType
bfdt = ml_dtypes.bfloat16

B, L, D = 4, 4096, 1024
H, DH = 16, 64
E = 32
HS = 512
T = 2048
NT = 16
NG = 32
DC = 8
HC = 36
TB = 1024
NTB = T // TB
C_EXP = 12.0
NREL = 2047
NRELX = 2175
PAD = 255
GW = 2560
EPS_LN = 1e-5
ZSLOT = 12

DEBUG_STAGE = 1


def win(lt):
    i0t = 256 * lt
    return max(0, i0t - 1023), min(L, i0t + 1279)


class Sem:
    def __init__(self, nc, name):
        self.h = nc.semaphore(name).__enter__()
        self.n = 0
    def inc(self, instr, k=1):
        self.n += k
        if instr is not None:
            instr.then_inc(self.h, k)
        return self.n


_DRAIN_OPS = {
    "memset", "memzero", "tensor_copy", "tensor_tensor", "tensor_add",
    "tensor_sub", "tensor_mul", "tensor_max", "tensor_scalar",
    "tensor_single_scalar", "tensor_scalar_add", "tensor_scalar_sub",
    "tensor_scalar_mul", "tensor_scalar_max", "tensor_scalar_min",
    "tensor_relu", "tensor_reduce", "reduce_max", "reduce_sum",
    "scalar_tensor_tensor", "select", "reciprocal", "activation", "square",
    "sqrt", "mul", "copy", "add", "sign", "affine_select", "iota",
    "bn_stats", "bn_aggr", "transpose", "stream_shuffle",
}


class Eng:
    def __init__(self, real, auto_drain=False):
        object.__setattr__(self, "real", real)
        object.__setattr__(self, "auto_drain", auto_drain)
        object.__setattr__(self, "_last_was_op", False)
    def __getattr__(self, name):
        real = object.__getattribute__(self, "real")
        if real is None:
            return lambda *a, **k: None
        fn = getattr(real, name)
        if name in _DRAIN_OPS and object.__getattribute__(self, "auto_drain"):
            def wrapped(*a, **k):
                if object.__getattribute__(self, "_last_was_op"):
                    real.drain()
                object.__setattr__(self, "_last_was_op", True)
                return fn(*a, **k)
            return wrapped
        return fn
    def wait(self, sem, val):
        real = object.__getattribute__(self, "real")
        if real is not None and val > 0:
            real.wait_ge(sem.h, val)


def build_program():
    nc = bass.Bass("TRN2", target_bir_lowering=False,
                   detect_race_conditions=os.environ.get("KRACE", "0") == "1")

    def din(name, shape, dtype=F32):
        return nc.dram_tensor(name, shape, dtype, kind="ExternalInput")

    x_in = din("x", [NT, 128, D])
    wq_d = din("wq", [DC, 128, DC * 128])
    wk_d = din("wk", [DC, 128, DC * 128])
    wv_d = din("wv", [D, D])
    wo_d = din("wo", [D, D])
    bq_d = din("bq", [1, D]); bk_d = din("bk", [1, D])
    bv_d = din("bv", [1, D]); bo_d = din("bo", [1, D])
    relr_d = din("relr", [128, NRELX])
    wr_d = din("wr", [128, DC * E])
    rb_d = din("rb", [E, 1])
    wg_d = din("wg", [HC, 128, DC * 128])
    wu_d = din("wu", [HC, 128, DC * 128])
    wd_d = din("wd", [2, HC, 128, 512])

    out_d = nc.dram_tensor("out", [NT, 128, D], F32, kind="ExternalOutput")

    k_hbm = nc.dram_tensor("k_hbm", [DC, 128, L], BF16, kind="Internal")
    v_hbm = nc.dram_tensor("v_hbm", [NG, 128, D], BF16, kind="Internal")
    g_hbm = nc.dram_tensor("g_hbm", [2, 128, 2 * GW], BF16, kind="Internal")
    ag_in = nc.dram_tensor("ag_in", [128, DC * T], BF16, kind="Internal")
    ag_out = nc.dram_tensor("ag_out", [2, 128, DC * T], BF16, kind="Internal")

    ESZ = {F32: 4, BF16: 2, U32: 4}

    def sbn(name, shape, dtype):
        return nc.alloc_sbuf_tensor(name, list(shape), dtype)

    X      = sbn("X", [128, NT * D], BF16)
    identb = sbn("identb", [128, 128], BF16)
    identf = sbn("identf", [128, 128], F32)
    onesr  = sbn("onesr", [1, 512], BF16)
    onestf = sbn("onestf", [128, 256], F32)
    onestb = sbn("onestb", [128, 288], BF16)
    sel2   = sbn("sel2", [2, 128], F32)
    biasq  = sbn("biasq", [1, D], BF16)
    biask  = sbn("biask", [1, D], BF16)
    biasv  = sbn("biasv", [1, D], BF16)
    biaso  = sbn("biaso", [1, D], BF16)
    rbias  = sbn("rbias", [E, 1], F32)
    relsb  = sbn("relsb", [128, NRELX], BF16)
    epsln  = sbn("epsln", [128, 1], F32)
    negCt  = sbn("negC", [128, 1], F32)
    TMPS   = sbn("TMPS", [128, 16], F32)

    ARENA_BYTES = 156 * 1024
    arena = nc.alloc_sbuf_tensor("ARENA", [128, ARENA_BYTES], mybir.dt.uint8)
    AB = nc.lookup_mloc(arena).addr
    cur = [AB]
    def sba(name, shape, dtype, at=None):
        per_part = int(np.prod(shape[1:])) * ESZ[dtype]
        if at is None:
            at = cur[0]
        else:
            at = AB + at
        cur[0] = max(cur[0], at + ((per_part + 31) // 32) * 32)
        assert cur[0] <= AB + ARENA_BYTES, (name, cur[0] - AB)
        return nc.alloc_sbuf_tensor_at(name, list(shape), dtype, offset=at)

    XLF  = sba("XLF", [128, DC * T], BF16)
    HFB_AT = cur[0] - AB
    HFB  = sba("HFB", [128, HC * TB], BF16)
    LNTOK = sba("LNTOK", [128, NT * D], BF16, at=HFB_AT)
    WGC  = sba("WGC", [128, 2 * DC * 128], BF16)
    WUC  = sba("WUC", [128, 2 * DC * 128], BF16)
    WDC  = sba("WDC", [128, 2 * 512], BF16)
    WRC  = sba("WRC", [128, DC * E], BF16)
    LOGF = sba("LOGF", [E, T], F32)
    LOGT = sba("LOGT", [128, NT * E], F32)
    LWORK = sba("LWORK", [128, E], F32)
    SELW = sba("SELW", [128, E], F32)
    MSCR = sba("MSCR", [128, E], F32)
    WTOK = sba("WTOK", [128, NT * E], F32)
    WTR  = sba("WTR", [E, T], BF16)
    eall = sba("eall", [32, E * 128], BF16)
    SGT  = sba("SGT", [128, TB], BF16)
    FFN_END = cur[0] - AB
    assert FFN_END <= ARENA_BYTES, FFN_END

    XF2A = sba("XF2A", [128, DC * T], BF16, at=0)
    XB0  = sba("XB0", [128, DC * T], BF16, at=32 * 1024)
    XB1  = sba("XB1", [128, DC * T], BF16, at=64 * 1024)
    QF   = sba("QF", [128, DC * T], BF16, at=96 * 1024)
    WVC  = sba("WVC", [128, DC * 1024], BF16, at=128 * 1024)
    WQKC = sba("WQKC", [128, 2 * DC * 128], BF16, at=144 * 1024)
    KVO  = sba("KVO", [128, 2 * 1024], BF16, at=148 * 1024)
    CTXF = sba("CTXF", [128, DC * T], BF16, at=0)
    KSB  = sba("KSB", [128, 2 * L], BF16, at=32 * 1024)
    VSB  = sba("VSB", [128, NG * 128], BF16, at=48 * 1024)
    ZACC = sba("ZACC", [128, 4 * ZSLOT], F32, at=56 * 1024)
    ZQT  = sba("ZQT", [128, 2], F32, at=56 * 1024 + 256)
    ZINVR = sba("ZINVR", [2, 128], F32, at=56 * 1024 + 288)
    BIASL = sba("BIASL", [128, 4], F32, at=56 * 1024 + 1024)
    BIASR = sba("BIASR", [128, 4], F32, at=56 * 1024 + 1056)
    ZBS   = sba("ZBS", [128, 128], F32, at=56 * 1024 + 1536)
    WOC  = sba("WOC", [128, 2 * 1024], BF16, at=58 * 1024)
    PSB  = sba("PSB", [128, 2 * L], BF16, at=64 * 1024)
    PTB  = sba("PTB", [128, 2 * L], BF16, at=80 * 1024)
    QREV = sba("QREV", [128, 2 * GW], BF16, at=128 * 1024)
    GPB  = sba("GPB", [128, 4 * 2304], BF16, at=138 * 1024)
    ATT_END = 156 * 1024
    assert ATT_END <= ARENA_BYTES

    pb = [nc.alloc_psum_tensor(f"pb{i}", [128, 512], F32) for i in range(8)]

    semnames = ["ld", "lwg0", "lwg1", "lwd0", "lwd1", "lwr", "st",
                "tc", "vc", "sc", "gc", "agS",
                "pt0", "pt1", "pt2", "pt3",
                "gw0", "gw1", "sk0", "sk1",
                "kv0", "kv1", "kvv", "ks0", "ks1", "ccS"]
    sems = {n: Sem(nc, n) for n in semnames}

    def program(which):
        t = Eng(nc.tensor if which == "t" else None)
        v = Eng(nc.vector if which == "v" else None, auto_drain=True)
        s = Eng(nc.scalar if which == "s" else None,
                auto_drain=os.environ.get("KRACE", "0") == "1")
        g = Eng(nc.gpsimd if which == "g" else None, auto_drain=True)
        y = Eng(nc.sync if which == "y" else None)
        for sm in sems.values():
            sm.n = 0
        ld, st = sems["ld"], sems["st"]
        lwg = [sems["lwg0"], sems["lwg1"]]
        lwd = [sems["lwd0"], sems["lwd1"]]
        lwr = sems["lwr"]
        agS = sems["agS"]
        ccS = sems["ccS"]
        pt = [sems["pt0"], sems["pt1"], sems["pt2"], sems["pt3"]]
        gw = [sems["gw0"], sems["gw1"]]
        sk = [sems["sk0"], sems["sk1"]]
        kvs = [sems["kv0"], sems["kv1"]]
        kvv = sems["kvv"]
        ks = [sems["ks0"], sems["ks1"]]
        tc, vc, sc, gc = sems["tc"], sems["vc"], sems["sc"], sems["gc"]

        def gdma(eng, sem, **kw):
            eng.wait(sem, sem.n)
            sem.inc(eng.dma_start(**kw), 16)

        for ins_args in [(onesr, 1.0), (onestf, 1.0), (onestb, 1.0),
                         (epsln, EPS_LN), (negCt, -C_EXP), (identb, 0.0),
                         (identf, 0.0), (sel2, 0.0), (eall, 0.0)]:
            gc.inc(g.memset(ins_args[0][:, :], ins_args[1]), 1)
        g.wait(gc, gc.n)
        gc.inc(g.affine_select(identb[:, :], onestb[:, 0:128],
                               [[-1, 128]], ALU.is_equal, 0.0,
                               base=0, channel_multiplier=1), 1)
        gc.inc(g.affine_select(identf[:, :], onestf[:, 0:128],
                               [[-1, 128]], ALU.is_equal, 0.0,
                               base=0, channel_multiplier=1), 1)
        gc.inc(g.affine_select(sel2[0:2, 0:64], onestf[0:2, 0:64], [[0, 64]],
                               ALU.is_equal, 0.0, base=0, channel_multiplier=1), 1)
        gc.inc(g.affine_select(sel2[0:2, 64:128], onestf[0:2, 0:64], [[0, 64]],
                               ALU.is_equal, 0.0, base=-1, channel_multiplier=1), 1)
        GC_INIT = gc.n

        for tt in range(NT):
            ld.inc(g.dma_start(out=X[:, tt * D:(tt + 1) * D], in_=x_in[tt]), 16)
        ld.inc(g.dma_start(out=relsb[:, :], in_=relr_d.ap()), 16)
        ld.inc(g.dma_start(out=rbias[:, :], in_=rb_d.ap()), 16)
        for bia, dsrc in [(biasq, bq_d), (biask, bk_d), (biasv, bv_d), (biaso, bo_d)]:
            ld.inc(g.dma_start(out=bia[:, :], in_=dsrc.ap()), 16)
        X_READY = ld.n

        def ln_all(wait_resid_vc, dst_f):
            s.wait(vc, wait_resid_vc)
            v.wait(vc, 0)
            for tt in range(NT):
                xs = X[:, tt * D:(tt + 1) * D]
                dst = LNTOK[:, tt * D:(tt + 1) * D]
                sc.inc(s.activation(dst, xs, AF.Square, accum_out=TMPS[:, 1:2]), 1)
                v.tensor_reduce(TMPS[:, 0:1], xs, AX.X, ALU.add, negate=True)
                v.wait(sc, sc.n)
                v.tensor_scalar(TMPS[:, 2:3], TMPS[:, 0:1], 1.0 / D, None, ALU.mult)
                v.tensor_tensor(TMPS[:, 3:4], TMPS[:, 2:3], TMPS[:, 2:3], ALU.mult)
                vc.inc(v.scalar_tensor_tensor(TMPS[:, 4:5], TMPS[:, 1:2], 1.0 / D,
                                              TMPS[:, 3:4], ALU.mult, ALU.subtract), 1)
                s.wait(vc, vc.n)
                sc.inc(s.activation(TMPS[:, 5:6], TMPS[:, 4:5], AF.Sqrt,
                                    bias=epsln[:, :]), 1)
                v.wait(sc, sc.n)
                v.reciprocal(TMPS[:, 6:7], TMPS[:, 5:6])
                vc.inc(v.tensor_scalar(dst, xs, TMPS[:, 2:3], TMPS[:, 6:7],
                                       ALU.add, ALU.mult), 1)
            LN_DONE = vc.n
            bank_free_at = [0, 0]
            for i in range(NT * DC):
                tt, c = divmod(i, DC)
                bk = pb[6 + (i % 2)]
                tdst = bk.ap().bitcast(BF16)[:, 0:128]
                if i == 0:
                    t.wait(vc, LN_DONE)
                t.wait(vc, bank_free_at[i % 2])
                tc.inc(t.transpose(tdst, LNTOK[:, tt * D + c * 128: tt * D + (c + 1) * 128],
                                   identb[:, :]), 1)
                v.wait(tc, tc.n)
                vc.inc(v.tensor_copy(dst_f[:, c * T + tt * 128: c * T + (tt + 1) * 128],
                                     tdst), 1)
                bank_free_at[i % 2] = vc.n
            return vc.n

        def ffn(call, wait_resid_vc):
            XLF_READY = ln_all(wait_resid_vc, XLF)

            gdma(g, lwr, out=WRC[:, :], in_=wr_d.ap())
            WR_LD = lwr.n
            t.wait(lwr, WR_LD)
            t.wait(vc, XLF_READY)
            for tkb in range(4):
                for dc in range(DC):
                    ins = t.matmul(pb[6][0:E, :], WRC[:, dc * E:(dc + 1) * E],
                                   XLF[:, dc * T + tkb * 512: dc * T + (tkb + 1) * 512],
                                   start=(dc == 0), stop=(dc == DC - 1))
                tc.inc(ins, 1)
                v.wait(tc, tc.n)
                vc.inc(v.tensor_scalar(LOGF[:, tkb * 512:(tkb + 1) * 512],
                                       pb[6][0:E, :], rbias[:, :], None, ALU.add), 1)
                t.wait(vc, vc.n)
            bank_free_at = [vc.n, vc.n]
            for tt in range(NT):
                bk = pb[6 + (tt % 2)]
                t.wait(vc, bank_free_at[tt % 2])
                tc.inc(t.transpose(bk.ap()[:, 0:E],
                                   LOGF[:, tt * 128:(tt + 1) * 128],
                                   identf[0:E, 0:E]), 1)
                v.wait(tc, tc.n)
                vc.inc(v.tensor_copy(LOGT[:, tt * E:(tt + 1) * E], bk.ap()[0:128, 0:E]), 1)
                bank_free_at[tt % 2] = vc.n

            for tt in range(NT):
                l0 = LOGT[:, tt * E:(tt + 1) * E]
                v.tensor_copy(LWORK[:, :], l0)
                v.memset(SELW[:, :], 0.0)
                for r in range(4):
                    v.tensor_reduce(TMPS[:, 8:9], LWORK[:, :], AX.X, ALU.max,
                                    negate=True)
                    v.tensor_scalar(MSCR[:, :], LWORK[:, :], TMPS[:, 8:9], 0.0,
                                    ALU.add, ALU.is_ge)
                    v.tensor_add(SELW[:, :], SELW[:, :], MSCR[:, :])
                    if r == 0:
                        v.tensor_copy(TMPS[:, 9:10], TMPS[:, 8:9])
                    if r < 3:
                        v.scalar_tensor_tensor(LWORK[:, :], MSCR[:, :], -30000.0,
                                               LWORK[:, :], ALU.mult, ALU.add)
                vc.inc(v.engine_nop(), 1)
                s.wait(vc, vc.n)
                sc.inc(s.activation(LWORK[:, :], l0, AF.Exp, bias=TMPS[:, 9:10]), 1)
                v.wait(sc, sc.n)
                v.tensor_mul(LWORK[:, :], LWORK[:, :], SELW[:, :])
                v.tensor_reduce(TMPS[:, 10:11], LWORK[:, :], AX.X, ALU.add)
                v.reciprocal(TMPS[:, 11:12], TMPS[:, 10:11])
                vc.inc(v.tensor_scalar(WTOK[:, tt * E:(tt + 1) * E], LWORK[:, :],
                                       TMPS[:, 11:12], None, ALU.mult), 1)
            bank_free_at = [vc.n, vc.n]
            for tt in range(NT):
                bk = pb[6 + (tt % 2)]
                t.wait(vc, bank_free_at[tt % 2])
                tc.inc(t.transpose(bk.ap()[0:E, 0:128],
                                   WTOK[:, tt * E:(tt + 1) * E], identf[:, :]), 1)
                v.wait(tc, tc.n)
                vc.inc(v.tensor_copy(WTR[:, tt * 128:(tt + 1) * 128],
                                     bk.ap()[0:E, 0:128]), 1)
                bank_free_at[tt % 2] = vc.n
            WTR_READY = vc.n

            g.wait(vc, wait_resid_vc)
            gc.inc(g.memset(eall[:, :], 0.0), 1)
            g.wait(gc, gc.n)
            for e in range(E):
                gc.inc(g.affine_select(eall[:, e * 128:(e + 1) * 128],
                                       onestb[0:32, 0:128], [[0, 128]],
                                       ALU.is_equal, 0.0,
                                       base=-e, channel_multiplier=1), 1)
            EALL_RDY = gc.n

            step_consumed = {}
            down_evac_end = vc.n
            for tb in range(NTB):
                tbo = tb * TB
                for hc in range(HC):
                    wpar = hc % 2
                    it = tb * HC + hc
                    if it >= 2:
                        g.wait(vc, step_consumed[2 * (it - 2) + 1])
                    g.wait(lwg[wpar], lwg[wpar].n)
                    lwg[wpar].inc(g.dma_start(out=WGC[:, wpar * 1024:(wpar + 1) * 1024],
                                       in_=wg_d[hc]), 16)
                    lwg[wpar].inc(g.dma_start(out=WUC[:, wpar * 1024:(wpar + 1) * 1024],
                                       in_=wu_d[hc]), 16)
                    W_LD = lwg[wpar].n
                    t.wait(lwg[wpar], W_LD)
                    if it == 0:
                        t.wait(gc, EALL_RDY)
                        t.wait(vc, WTR_READY)
                        t.wait(vc, down_evac_end)
                    for th in range(2):
                        stp = 2 * it + th
                        sl = stp % 2
                        gbank, ubank, wbank = pb[0 + 2 * sl], pb[1 + 2 * sl], pb[4 + sl]
                        if stp >= 2:
                            t.wait(vc, step_consumed[stp - 2])
                        mvx = XLF
                        co = tbo + th * 512
                        if hc < E:
                            t.matmul(wbank.ap(), eall[:, hc * 128:(hc + 1) * 128],
                                     WTR[0:E, co:co + 512], start=True, stop=True)
                        for dc in range(DC):
                            t.matmul(gbank.ap(),
                                     WGC[:, wpar * 1024 + dc * 128: wpar * 1024 + (dc + 1) * 128],
                                     mvx[:, dc * T + co: dc * T + co + 512],
                                     start=(dc == 0), stop=(dc == DC - 1))
                        glast = None
                        for dc in range(DC):
                            glast = t.matmul(ubank.ap(),
                                     WUC[:, wpar * 1024 + dc * 128: wpar * 1024 + (dc + 1) * 128],
                                     mvx[:, dc * T + co: dc * T + co + 512],
                                     start=(dc == 0), stop=(dc == DC - 1))
                        tc.inc(glast, 1)
                        GU_DONE = tc.n
                        s.wait(tc, GU_DONE)
                        sc.inc(s.activation(SGT[:, th * 512:(th + 1) * 512],
                                            gbank.ap(), AF.Silu), 1)
                        v.wait(sc, sc.n)
                        hss = HFB[:, hc * TB + th * 512: hc * TB + (th + 1) * 512]
                        v.scalar_tensor_tensor(hss, ubank.ap(), 1.0,
                                               SGT[:, th * 512:(th + 1) * 512],
                                               ALU.mult, ALU.mult)
                        if hc < E:
                            vins = v.tensor_tensor(hss, hss, wbank.ap(), ALU.mult)
                        else:
                            vins = v.engine_nop()
                        vc.inc(vins, 1)
                        step_consumed[stp] = vc.n
                H_DONE = vc.n

                wd_consumed = {}
                for half in range(2):
                    for hc in range(HC):
                        wpar = hc % 2
                        ih = half * HC + hc
                        if ih >= 2:
                            g.wait(tc, wd_consumed[ih - 2])
                        gdma(g, lwd[wpar], out=WDC[:, wpar * 512:(wpar + 1) * 512],
                             in_=wd_d[half, hc])
                        WD_LD = lwd[wpar].n
                        t.wait(lwd[wpar], WD_LD)
                        if hc == 0:
                            t.wait(vc, H_DONE if half == 0 else vc.n)
                        ins = None
                        for tt8 in range(8):
                            ins = t.matmul(pb[tt8].ap(),
                                           HFB[:, hc * TB + tt8 * 128:
                                               hc * TB + (tt8 + 1) * 128],
                                           WDC[:, wpar * 512:(wpar + 1) * 512],
                                           start=(hc == 0), stop=(hc == HC - 1))
                        tc.inc(ins, 1)
                        wd_consumed[ih] = tc.n
                    DOWN_DONE = tc.n
                    v.wait(tc, DOWN_DONE)
                    for tt8 in range(8):
                        gt = tb * 8 + tt8
                        xsl = X[:, gt * D + half * 512: gt * D + (half + 1) * 512]
                        vc.inc(v.scalar_tensor_tensor(xsl, pb[tt8].ap(), 0.5,
                                                      xsl, ALU.mult, ALU.add), 1)
                    down_evac_end = vc.n
                    t.wait(vc, down_evac_end)
            return vc.n

        def attention(wait_resid_vc):
            XF2_READY = ln_all(wait_resid_vc, XF2A)

            g.wait(vc, XF2_READY)
            agS.inc(g.dma_start(out=ag_in.ap(), in_=XF2A[:, :]), 16)
            g.wait(agS, agS.n)
            ccS.inc(g.collective_compute(
                "AllGather", ALU.bypass,
                replica_groups=[[0, 1], [2, 3], [4, 5], [6, 7]],
                ins=[ag_in.ap().opt()], outs=[ag_out.ap().opt()]), 1)
            g.wait(ccS, ccS.n)
            agS.inc(g.dma_start(out=XB0[:, :], in_=ag_out[0]), 16)
            agS.inc(g.dma_start(out=XB1[:, :], in_=ag_out[1]), 16)
            XF2B_READY = agS.n

            t.wait(agS, XF2B_READY)
            t.wait(vc, XF2_READY)
            gdma(g, lwr,
                 out=bass.AP(WVC, 0, [[DC * 1024, 128], [1024, DC], [1, 1024]]),
                 in_=bass.AP(wv_d, 0, [[1024, 128], [131072, DC], [1, 1024]]))
            WVC_LD = lwr.n
            qk_cons = {}
            for qc in range(DC):
                wpar = qc % 2
                if qc >= 2:
                    g.wait(tc, qk_cons[qc - 2])
                gdma(g, lwg[wpar],
                     out=WQKC[:, wpar * 1024:(wpar + 1) * 1024], in_=wq_d[qc])
                t.wait(lwg[wpar], lwg[wpar].n)
                for tkb in range(4):
                    bk = pb[tkb % 4]
                    for dc in range(DC):
                        t.matmul(bk.ap(),
                                 WQKC[:, wpar * 1024 + dc * 128: wpar * 1024 + (dc + 1) * 128],
                                 XF2A[:, dc * T + tkb * 512: dc * T + (tkb + 1) * 512],
                                 start=(dc == 0), stop=False)
                    tc.inc(t.matmul(bk.ap(), biasq[0:1, qc * 128:(qc + 1) * 128],
                                    onesr[0:1, 0:512], start=False, stop=True), 1)
                    v.wait(tc, tc.n)
                    vc.inc(v.tensor_copy(
                        QF[:, qc * T + tkb * 512: qc * T + (tkb + 1) * 512], bk.ap()), 1)
                    t.wait(vc, vc.n)
                qk_cons[qc] = tc.n
            QF_READY = vc.n
            kv_ev = 0
            for qc in range(DC):
                wpar = qc % 2
                g.wait(tc, qk_cons[DC + qc - 2] if qc >= 2 else qk_cons[DC - 1])
                gdma(g, lwg[wpar],
                     out=WQKC[:, wpar * 1024:(wpar + 1) * 1024], in_=wk_d[qc])
                t.wait(lwg[wpar], lwg[wpar].n)
                for half in range(2):
                    xsrc = XB0 if half == 0 else XB1
                    for grp in range(4):
                        bk = pb[(half * 4 + grp) % 4]
                        for dc in range(DC):
                            t.matmul(bk.ap(),
                                     WQKC[:, wpar * 1024 + dc * 128: wpar * 1024 + (dc + 1) * 128],
                                     xsrc[:, dc * T + grp * 512: dc * T + (grp + 1) * 512],
                                     start=(dc == 0), stop=False)
                        tc.inc(t.matmul(bk.ap(), biask[0:1, qc * 128:(qc + 1) * 128],
                                        onesr[0:1, 0:512], start=False, stop=True), 1)
                        kb = kv_ev % 2
                        v.wait(tc, tc.n)
                        v.wait(ks[kb], ks[kb].n)
                        vc.inc(v.tensor_scalar(KVO[:, kb * 1024: kb * 1024 + 512],
                                               bk.ap(), 0.125, None, ALU.mult), 1)
                        t.wait(vc, vc.n)
                        g.wait(vc, vc.n)
                        off = qc * 524288 + grp * 1024 + half * 128
                        gdma(g, ks[kb],
                             out=bass.AP(k_hbm, off, [[4096, 128], [256, 4], [1, 128]]),
                             in_=bass.AP(KVO, kb * 1024,
                                         [[2048, 128], [128, 4], [1, 128]]))
                        kv_ev += 1
                qk_cons[DC + qc] = tc.n
            t.wait(lwr, WVC_LD)
            for half in range(2):
                xsrc = XB0 if half == 0 else XB1
                for lt in range(NT):
                    bks = [pb[(2 * lt) % 4], pb[(2 * lt) % 4 + 1]]
                    for vh in range(2):
                        bk = bks[vh]
                        for dc in range(DC):
                            t.matmul(bk.ap(),
                                     xsrc[:, dc * T + lt * 128: dc * T + (lt + 1) * 128],
                                     WVC[:, dc * 1024 + vh * 512: dc * 1024 + (vh + 1) * 512],
                                     start=(dc == 0), stop=False)
                        tc.inc(t.matmul(bk.ap(), onesr[0:1, 0:128],
                                        biasv[0:1, vh * 512:(vh + 1) * 512],
                                        start=False, stop=True), 1)
                    kb = kv_ev % 2
                    v.wait(tc, tc.n)
                    v.wait(ks[kb], ks[kb].n)
                    v.tensor_copy(KVO[:, kb * 1024: kb * 1024 + 512], bks[0].ap())
                    vc.inc(v.tensor_copy(KVO[:, kb * 1024 + 512: kb * 1024 + 1024],
                                         bks[1].ap()), 1)
                    t.wait(vc, vc.n)
                    g.wait(vc, vc.n)
                    gdma(g, ks[kb],
                         out=v_hbm[2 * lt + half],
                         in_=KVO[:, kb * 1024: kb * 1024 + 1024])
                    kv_ev += 1
            KS_FIN = [ks[0].n, ks[1].n]
            QKV_T_DONE = tc.n

            g.wait(ks[0], KS_FIN[0])
            g.wait(ks[1], KS_FIN[1])

            t.wait(vc, QF_READY)
            C_QREL = [(c * 512, min(NRELX, (c + 1) * 512)) for c in range(5)]
            st_qrel_vc = 0
            st_gw = 0
            st_relmm = {0: 0, 1: 0}
            st_ctx_free = {0: 0, 1: 0}
            st_pb5_free = 0
            st_last_pv = 0
            st_trp_free = {0: 0, 1: 0}
            st_psb_free = {0: 0, 1: 0}
            st_ptb_free = 0
            ksb_free = {}
            vsb_free = {}

            for hp in range(8):
                kb = hp % 2
                if hp >= 2:
                    g.wait(tc, ksb_free[hp - 2])
                gdma(g, kvs[kb], out=KSB[:, kb * L:(kb + 1) * L], in_=k_hbm[hp])
                KSB_LD = kvs[kb].n
                if hp >= 1:
                    g.wait(tc, vsb_free[hp - 1])
                gdma(g, kvv,
                     out=bass.AP(VSB, 0, [[NG * 128, 128], [128, NG], [1, 128]]),
                     in_=bass.AP(v_hbm, hp * 128,
                                 [[1024, 128], [131072, NG], [1, 128]]))
                VSB_LD = kvv.n

                qrel_ev = {}
                sk_mark = {}
                exp_mark = {}
                pv_last = {}
                ctx_ev = {}
                ptb_ready = {}

                def emit_qrel(lt):
                    nonlocal st_qrel_vc, st_gw
                    qbuf = lt % 2
                    jlo, jhi = win(lt)
                    v.wait(gw[0], st_gw)
                    for hip in range(2):
                        for c0, c1 in C_QREL:
                            t.wait(vc, st_qrel_vc)
                            tc.inc(t.matmul(
                                pb[4][:, 0:c1 - c0],
                                QF[hip * 64:(hip + 1) * 64,
                                   hp * T + lt * 128: hp * T + (lt + 1) * 128],
                                relsb[hip * 64:(hip + 1) * 64, c0:c1],
                                start=True, stop=True,
                                tile_position=(hip * 64, 0)), 1)
                            v.wait(tc, tc.n)
                            st_qrel_vc = vc.inc(v.tensor_copy(
                                QREV[:, hip * GW + PAD + c0: hip * GW + PAD + c1],
                                pb[4][:, 0:c1 - c0]), 1)
                        v.tensor_copy(TMPS[:, 12:13],
                                      QREV[:, hip * GW + PAD: hip * GW + PAD + 1])
                        v.tensor_copy(TMPS[:, 13:14],
                                      QREV[:, hip * GW + PAD + NRELX - 1: hip * GW + PAD + NRELX])
                        v.tensor_scalar(QREV[:, hip * GW: hip * GW + PAD],
                                        onestb[:, 0:PAD],
                                        TMPS[:, 12:13], None, ALU.mult)
                        v.tensor_scalar(
                            QREV[:, hip * GW + PAD + NRELX: hip * GW + GW],
                            onestb[:, 0:GW - PAD - NRELX],
                            TMPS[:, 13:14], None, ALU.mult)
                        v.tensor_scalar(BIASL[:, qbuf * 2 + hip: qbuf * 2 + hip + 1],
                                        QREV[:, hip * GW + PAD: hip * GW + PAD + 1],
                                        -C_EXP, None, ALU.add)
                        v.tensor_scalar(BIASR[:, qbuf * 2 + hip: qbuf * 2 + hip + 1],
                                        QREV[:, hip * GW + PAD + NRELX - 1: hip * GW + PAD + NRELX],
                                        -C_EXP, None, ALU.add)
                    qrel_ev[lt] = vc.inc(v.engine_nop(), 1)
                    g.wait(vc, qrel_ev[lt])
                    g.wait(sk[qbuf], sk[qbuf].n)
                    st_gw = gw[0].inc(g.dma_start(out=g_hbm[qbuf], in_=QREV[:, :]), 16)

                def emit_skew(lt):
                    qbuf = lt % 2
                    jlo, jhi = win(lt)
                    g.wait(gw[0], st_gw)
                    g.wait(tc, st_relmm[qbuf])
                    i0c = 256 * lt
                    W = jhi - jlo
                    g.wait(sk[qbuf], sk[qbuf].n)
                    for hip in range(2):
                        base = qbuf * 128 * (2 * GW) + hip * GW + (1278 - i0c + jlo)
                        sk[qbuf].inc(g.dma_start(
                            out=GPB[:, (qbuf * 2 + hip) * 2304: (qbuf * 2 + hip) * 2304 + W],
                            in_=bass.AP(g_hbm, base,
                                        [[2 * GW - 1, 128], [1, W]])), 16)
                    sk_mark[lt] = sk[qbuf].n

                def emit_pv_chunk(lq, kc):
                    cbk = pb[6 + lq % 2]
                    if kc == 0:
                        t.wait(vc, ptb_ready[lq])
                        t.wait(kvv, VSB_LD)
                        t.wait(vc, st_ctx_free[lq % 2])
                    t.matmul(cbk[0:64, 0:128],
                             VSB[:, kc * 128: kc * 128 + 64],
                             PTB[:, 0 * L + kc * 128: 0 * L + (kc + 1) * 128],
                             start=(kc == 0), stop=(kc == NG - 1),
                             tile_position=(0, 0), skip_group_check=True)
                    lastm = t.matmul(cbk[64:128, 0:128],
                             VSB[:, kc * 128 + 64: (kc + 1) * 128],
                             PTB[:, 1 * L + kc * 128: 1 * L + (kc + 1) * 128],
                             start=(kc == 0), stop=(kc == NG - 1),
                             tile_position=(0, 64), skip_group_check=True)
                    if kc == NG - 1:
                        pv_last[lq] = tc.inc(lastm, 1)
                        st_last_pv_set(pv_last[lq])

                def st_last_pv_set(vv):
                    nonlocal st_last_pv
                    st_last_pv = vv

                def emit_z(lq):
                    nonlocal st_pb5_free
                    qb = lq % 2
                    cbk = pb[6 + qb]
                    v.wait(sc, exp_all_m[(1, lq)])
                    v.tensor_reduce(ZQT[:, 0:1],
                                    ZACC[:, qb * 2 * ZSLOT: qb * 2 * ZSLOT + ZSLOT],
                                    AX.X, ALU.add)
                    v.tensor_reduce(ZQT[:, 1:2],
                                    ZACC[:, qb * 2 * ZSLOT + ZSLOT: (qb + 1) * 2 * ZSLOT],
                                    AX.X, ALU.add)
                    zr = vc.inc(v.tensor_scalar(ZQT[:, :], ZQT[:, :],
                                                1.0 + 1e-8, None, ALU.mult), 1)
                    t.wait(vc, zr)
                    t.wait(vc, st_pb5_free)
                    zt = tc.inc(t.transpose(pb[5][0:2, 0:128], ZQT[:, :],
                                            identf[:, :]), 1)
                    v.wait(tc, zt)
                    rcp = vc.inc(v.reciprocal(ZINVR[0:2, :], pb[5][0:2, 0:128]), 1)
                    t.wait(vc, rcp)
                    zb = tc.inc(t.matmul(pb[5][:, 128:256], sel2[0:2, :],
                                         ZINVR[0:2, :], start=True, stop=True), 1)
                    v.wait(tc, zb)
                    v.wait(tc, pv_last[lq])
                    v.tensor_copy(ZBS[:, :], pb[5][:, 128:256])
                    ctx_ev[lq] = vc.inc(v.tensor_tensor(
                        CTXF[:, hp * T + lq * 128: hp * T + (lq + 1) * 128],
                        cbk[:, 0:128], ZBS[:, :], ALU.mult), 1)
                    st_ctx_free[qb] = ctx_ev[lq]
                    st_pb5_free = ctx_ev[lq]

                emit_qrel(0)
                emit_skew(0)
                emit_qrel(1)
                emit_skew(1)
                exp_all_m = {}

                for qt in range(NT):
                    qbuf = qt % 2
                    jlo, jhi = win(qt)
                    if qt + 2 < NT:
                        emit_qrel(qt + 2)
                    sc0 = sc.n
                    zm = vc.inc(v.memset(ZACC[:, qbuf * 2 * ZSLOT:
                                              (qbuf + 1) * 2 * ZSLOT], 0.0), 1)
                    for stp in range(16):
                        kc, hip = stp // 2, stp % 2
                        c0, c1 = kc * 512, (kc + 1) * 512
                        b0, b1 = max(c0, jlo), min(c1, jhi)
                        bkS = pb[stp % 2]
                        t.wait(sc, sc0 + stp - 1)
                        if stp == 0:
                            t.wait(kvs[kb], KSB_LD)
                            t.wait(sk[qbuf], sk_mark[qt])
                        lastm = t.matmul(
                            bkS.ap(),
                            QF[hip * 64:(hip + 1) * 64,
                               hp * T + qt * 128: hp * T + (qt + 1) * 128],
                            KSB[hip * 64:(hip + 1) * 64, kb * L + c0: kb * L + c1],
                            start=True, stop=(b0 >= b1),
                            tile_position=(hip * 64, 0))
                        if b0 < b1:
                            lastm = t.matmul(
                                bkS[:, b0 - c0: b1 - c0],
                                identb[:, :],
                                GPB[:, (qbuf * 2 + hip) * 2304 + (b0 - jlo):
                                    (qbuf * 2 + hip) * 2304 + (b1 - jlo)],
                                start=False, stop=True)
                        exp_mark[(hip, qt, kc)] = tc.inc(lastm, 1)
                        if qt >= 1:
                            emit_pv_chunk(qt - 1, 2 * stp)
                            emit_pv_chunk(qt - 1, 2 * stp + 1)
                    st_relmm[qbuf] = tc.n

                    s.wait(vc, zm)
                    s.wait(vc, qrel_ev[qt])
                    s.wait(tc, st_psb_free[0])
                    s.wait(tc, st_psb_free[1])
                    slot = [0, 0]
                    for kc in range(8):
                        c0, c1 = kc * 512, (kc + 1) * 512
                        for hip in range(2):
                            bkS = pb[(kc * 2 + hip) % 2]
                            s.wait(tc, exp_mark[(hip, qt, kc)])
                            segs = []
                            if c0 < jlo:
                                segs.append((c0, min(c1, jlo),
                                             BIASL[:, qbuf * 2 + hip: qbuf * 2 + hip + 1]))
                            if max(c0, jlo) < min(c1, jhi):
                                segs.append((max(c0, jlo), min(c1, jhi), negCt[:, :]))
                            if jhi < c1:
                                segs.append((max(c0, jhi), c1,
                                             BIASR[:, qbuf * 2 + hip: qbuf * 2 + hip + 1]))
                            last = None
                            for (s0, s1, bias_ap) in segs:
                                last = s.activation(
                                    PSB[:, hip * L + s0: hip * L + s1],
                                    bkS[:, s0 - c0: s1 - c0],
                                    AF.Exp, bias=bias_ap,
                                    accum_out=ZACC[:, qbuf * 2 * ZSLOT + hip * ZSLOT + slot[hip]:
                                                   qbuf * 2 * ZSLOT + hip * ZSLOT + slot[hip] + 1])
                                slot[hip] += 1
                            sc.inc(last, 1)
                    exp_all_m[(0, qt)] = sc.n
                    exp_all_m[(1, qt)] = sc.n

                    if qt >= 1:
                        emit_z(qt - 1)

                    for hip in range(2):
                        for j4 in range(NG // 4):
                            bkT = pb[2 + j4 % 2]
                            t.wait(vc, st_trp_free[j4 % 2])
                            t.wait(sc, sc0 + 2 * j4 + hip + 1)
                            if hip == 0 and j4 == 0:
                                t.wait(tc, pv_last[qt - 1] if qt >= 1 else st_ptb_free)
                            lastm = None
                            for jj in range(4):
                                j = j4 * 4 + jj
                                lastm = t.transpose(
                                    bkT.ap().bitcast(BF16)[:, jj * 128:(jj + 1) * 128],
                                    PSB[:, hip * L + j * 128: hip * L + (j + 1) * 128],
                                    identb[:, :])
                            tc.inc(lastm, 1)
                            if j4 == NG // 4 - 1:
                                st_psb_free[hip] = tc.n
                            v.wait(tc, tc.n)
                            st_trp_free[j4 % 2] = vc.inc(v.tensor_copy(
                                PTB[:, hip * L + j4 * 512: hip * L + (j4 + 1) * 512],
                                bkT.ap().bitcast(BF16)[:, 0:512]), 1)
                    ptb_ready[qt] = vc.n
                    if qt + 2 < NT:
                        emit_skew(qt + 2)

                for kc in range(NG):
                    emit_pv_chunk(NT - 1, kc)
                emit_z(NT - 1)
                st_ptb_free = pv_last[NT - 1]
                ksb_free[hp] = tc.n
                vsb_free[hp] = tc.n

            op_bank_free = {}
            for grp in range(2):
                for half in range(2):
                    for dc in range(DC):
                        wpar = dc % 2
                        gi = (grp * 2 + half) * DC + dc
                        g.wait(tc, op_bank_free.get(gi - 2, 0))
                        gdma(g, lwg[wpar],
                             out=WOC[:, wpar * 1024: wpar * 1024 + 512],
                             in_=wo_d.ap()[dc * 128:(dc + 1) * 128,
                                           half * 512:(half + 1) * 512])
                        t.wait(lwg[wpar], lwg[wpar].n)
                        if dc == 0:
                            t.wait(vc, vc.n)
                        for q8 in range(8):
                            qt = grp * 8 + q8
                            lastm = t.matmul(
                                pb[q8].ap(),
                                CTXF[:, dc * T + qt * 128: dc * T + (qt + 1) * 128],
                                WOC[:, wpar * 1024: wpar * 1024 + 512],
                                start=(dc == 0), stop=False)
                        if dc == DC - 1:
                            for q8 in range(8):
                                lastm = t.matmul(pb[q8].ap(), onesr[0:1, 0:128],
                                                 biaso[0:1, half * 512:(half + 1) * 512],
                                                 start=False, stop=True)
                        tc.inc(lastm, 1)
                        op_bank_free[gi] = tc.n
                    v.wait(tc, tc.n)
                    for q8 in range(8):
                        qt = grp * 8 + q8
                        xsl = X[:, qt * D + half * 512: qt * D + (half + 1) * 512]
                        vc.inc(v.tensor_tensor(xsl, xsl, pb[q8].ap(), ALU.add), 1)
                    t.wait(vc, vc.n)
            return vc.n

        t.wait(gc, GC_INIT); v.wait(gc, GC_INIT); s.wait(gc, GC_INIT); y.wait(gc, GC_INIT)
        t.wait(ld, X_READY); v.wait(ld, X_READY); s.wait(ld, X_READY)
        resid_vc = 0
        resid_vc = ffn(0, resid_vc)
        resid_vc = attention(resid_vc)
        resid_vc = ffn(1, resid_vc)

        g.wait(vc, resid_vc)
        g.wait(tc, sems["tc"].n)
        for tt in range(NT):
            st.inc(g.dma_start(out=out_d[tt], in_=X[:, tt * D:(tt + 1) * D]), 16)
        g.wait(st, st.n)

    with nc.Block() as blk:
        blk.tensor(lambda e: program("t"))
        blk.vector(lambda e: program("v"))
        blk.scalar(lambda e: program("s"))
        blk.gpsimd(lambda e: program("g"))
        blk.sync(lambda e: program("y"))
    return nc


def _prep_weights(inputs):
    f32 = np.float32
    wq = np.ascontiguousarray(inputs["L0_attn_Wq"], dtype=f32)
    wk = np.ascontiguousarray(inputs["L0_attn_Wk"], dtype=f32)

    def qk_layout(w):
        w4 = w.reshape(DC, 128, DC, 128)
        return np.ascontiguousarray(w4.transpose(2, 1, 0, 3).reshape(DC, 128, DC * 128))

    gate = np.asarray(inputs["L0_ffn_gate_w"], dtype=f32).reshape(4096, D)
    up = np.asarray(inputs["L0_ffn_up_w"], dtype=f32).reshape(4096, D)
    sgate = np.asarray(inputs["L0_ffn_sgate_w"], dtype=f32)
    sup = np.asarray(inputs["L0_ffn_sup_w"], dtype=f32)
    down = np.asarray(inputs["L0_ffn_down_w"], dtype=f32).transpose(0, 2, 1).reshape(4096, D)
    sdown = np.asarray(inputs["L0_ffn_sdown_w"], dtype=f32).T

    gall = np.concatenate([gate, sgate], 0)
    uall = np.concatenate([up, sup], 0)
    dall = np.concatenate([down, sdown], 0)

    def gu_layout(w):
        w4 = w.reshape(HC, 128, DC, 128)
        w4 = w.reshape(HC, 128, DC, 128)
        return np.ascontiguousarray(w4.transpose(0, 3, 2, 1).reshape(HC, 128, DC * 128))

    def down_layout(w):
        w4 = w.reshape(HC, 128, 2, 512)
        return np.ascontiguousarray(w4.transpose(2, 0, 1, 3))

    router = np.asarray(inputs["L0_ffn_router_w"], dtype=f32)
    wr = np.ascontiguousarray(router.reshape(E, DC, 128).transpose(2, 1, 0)
                              .reshape(128, DC * E))

    rel = np.asarray(inputs["L0_attn_rel_emb"], dtype=f32)
    rel_rev = rel[::-1]
    relx = {}
    for par in (0, 1):
        idx = np.clip(np.arange(NRELX) - 128 * par, 0, NREL - 1)
        rx = rel_rev[idx].T
        relx[par] = np.ascontiguousarray(np.concatenate([rx, rx], axis=0))

    out = {
        "wq": qk_layout(wq), "wk": qk_layout(wk),
        "wv": np.ascontiguousarray(inputs["L0_attn_Wv"], dtype=f32),
        "wo": np.ascontiguousarray(inputs["L0_attn_Wo"], dtype=f32),
        "bq": np.ascontiguousarray(inputs["L0_attn_bq"], dtype=f32).reshape(1, D),
        "bk": np.ascontiguousarray(inputs["L0_attn_bk"], dtype=f32).reshape(1, D),
        "bv": np.ascontiguousarray(inputs["L0_attn_bv"], dtype=f32).reshape(1, D),
        "bo": np.ascontiguousarray(inputs["L0_attn_bo"], dtype=f32).reshape(1, D),
        "wr": wr,
        "rb": np.ascontiguousarray(inputs["L0_ffn_router_bias"], dtype=f32).reshape(E, 1),
        "wg": gu_layout(gall), "wu": gu_layout(uall), "wd": down_layout(dall),
    }
    return out, relx


_NC_CACHE = {}

def kernel(**inputs):
    if "prog" not in _NC_CACHE:
        _NC_CACHE["prog"] = build_program()
    nc = _NC_CACHE["prog"]
    w, relx = _prep_weights(inputs)
    x = np.asarray(inputs["L0_attn_x"], dtype=np.float32)
    in_maps = []
    for c in range(8):
        b, par = c // 2, c % 2
        xt = np.ascontiguousarray(
            x[b].reshape(NG, 128, D)[par::2])
        im = dict(w)
        im["x"] = xt
        im["relr"] = relx[par]
        in_maps.append(im)
    res = run_bass_kernel_spmd(nc, in_maps, core_ids=list(range(8)))
    out = np.zeros((B, NG, 128, D), dtype=np.float32)
    for c in range(8):
        b, par = c // 2, c % 2
        out[b, par::2] = res.results[c]["out"]
    return out.reshape(B, L, D)
